# version 06 (final)
# baseline (speedup 1.0000x reference)
import math
import dataclasses
import numpy as np
import ml_dtypes
from contextlib import ExitStack

import concourse.bass as bass
import concourse.mybir as mybir
from concourse.expressions import RuntimeValue
from concourse.bass_utils import run_bass_kernel_spmd

F32 = mybir.dt.float32
BF16 = mybir.dt.bfloat16
U32 = mybir.dt.uint32
AF = mybir.ActivationFunctionType
AL = mybir.AluOpType
AX = mybir.AxisListType

D = 2048
DCH = 16
TLOC = 1024
TGLB = 2048
NHA = 8
NGR = 4
NST = 8
DFF = 5632
HCH = 44
CEXP = 20.0
SCA = 1.0 / math.sqrt(128.0)
SCB = 1.0 / math.sqrt(64.0)
WCNT = float(DFF * D)
VW = 136

ENGS = ("sync", "tensor", "vector", "scalar", "gpsimd")


class Ctr:
    __slots__ = ("h", "v")

    def __init__(self, h):
        self.h = h
        self.v = 0

    def inc(self, n=1):
        self.v += n
        return self.v


def _bcast_ap(a, n=128):
    return dataclasses.replace(a, ap=[[0, n]] + list(a.ap[1:]))


def build(n_sub=4, dbg=False, no_quant=False, dbg_units=None):
    nc = bass.Bass("TRN2", target_bir_lowering=False, debug=False)

    ap = lambda t: t.ap()
    xt_d = ap(nc.dram_tensor("xt", [D, TLOC], F32, kind="ExternalInput"))
    tok0_d = ap(nc.dram_tensor("tok0", [1, 1], U32, kind="ExternalInput"))
    wpq_d = ap(nc.dram_tensor("wpq", [D, 1024], F32, kind="ExternalInput"))
    wpk_d = ap(nc.dram_tensor("wpk", [D, 1024], F32, kind="ExternalInput"))
    wpv_d = ap(nc.dram_tensor("wpv", [D, 1024], F32, kind="ExternalInput"))
    woa_d = ap(nc.dram_tensor("woa", [D, D], F32, kind="ExternalInput"))
    wqb_d = ap(nc.dram_tensor("wqb", [D, 1024], F32, kind="ExternalInput"))
    wkb_d = ap(nc.dram_tensor("wkb", [D, 512], F32, kind="ExternalInput"))
    wvb_d = ap(nc.dram_tensor("wvb", [D, 256], F32, kind="ExternalInput"))
    wob_d = ap(nc.dram_tensor("wob", [D, D], F32, kind="ExternalInput"))
    w_sl_d = [ap(nc.dram_tensor("w1s", [256, DFF], F32, kind="ExternalInput")),
              ap(nc.dram_tensor("w2s", [256, DFF], F32, kind="ExternalInput")),
              ap(nc.dram_tensor("w3s", [704, D], F32, kind="ExternalInput"))]
    mask_d = ap(nc.dram_tensor("maskab", [128, 512], BF16, kind="ExternalInput"))
    out_d = ap(nc.dram_tensor("out_xt", [D, TLOC], F32, kind="ExternalOutput"))
    dbg_d = [ap(nc.dram_tensor(f"dbg{i}", [D, TLOC], F32, kind="ExternalOutput"))
             for i in range(4)] if dbg else []
    if dbg:
        t_yg_d = ap(nc.dram_tensor("t_yg", [128, DCH * TGLB], BF16,
                                   kind="ExternalOutput"))
        t_qh_d = ap(nc.dram_tensor("t_qh", [128, TGLB], BF16,
                                   kind="ExternalOutput"))
        t_kh_d = ap(nc.dram_tensor("t_kh", [128, TGLB], BF16,
                                   kind="ExternalOutput"))
        t_vh_d = ap(nc.dram_tensor("t_vh", [128, DCH * VW], BF16,
                                   kind="ExternalOutput"))
        t_cx_d = ap(nc.dram_tensor("t_cx", [128, TGLB], BF16,
                                   kind="ExternalOutput"))
        t_p_d = ap(nc.dram_tensor("t_p", [128, 4 * 512], BF16,
                                  kind="ExternalOutput"))
        t_ci_d = ap(nc.dram_tensor("t_ci", [1024, TGLB], BF16,
                                   kind="ExternalOutput"))
        t_o_d = ap(nc.dram_tensor("t_o", [128, 512], F32,
                                  kind="ExternalOutput"))
        t_o2_d = ap(nc.dram_tensor("t_o2", [128, 512], F32,
                                   kind="ExternalOutput"))
        t_xm_d = ap(nc.dram_tensor("t_xm", [128, 512], F32,
                                   kind="ExternalOutput"))
        t_wo_d = ap(nc.dram_tensor("t_wo", [128, DCH * 448], BF16,
                                   kind="ExternalOutput"))
        t_cr_d = ap(nc.dram_tensor("t_cr", [128, DCH * 1024], BF16,
                                   kind="ExternalOutput"))
        t_kh2_d = ap(nc.dram_tensor("t_kh2", [128, TGLB], BF16,
                                    kind="ExternalOutput"))
        t_thr_d = ap(nc.dram_tensor("t_thr", [128, 8], F32,
                                    kind="ExternalOutput"))
        t_ws_d = ap(nc.dram_tensor("t_ws", [128, 4], F32,
                                   kind="ExternalOutput"))
        t_wq_d = ap(nc.dram_tensor("t_wq", [128, DFF], BF16,
                                   kind="ExternalOutput"))
        t_h_d = ap(nc.dram_tensor("t_h", [128, 512], BF16,
                                  kind="ExternalOutput"))
        t_x1_d = ap(nc.dram_tensor("t_x1", [128, 512], F32,
                                   kind="ExternalOutput"))

    y_agi = nc.dram_tensor("y_agi", [D, TLOC], BF16)
    y_ago = nc.dram_tensor("y_ago", [2 * D, TLOC], BF16)
    c_agi = nc.dram_tensor("c_agi", [1024, TGLB], BF16)
    c_ago = nc.dram_tensor("c_ago", [D, TGLB], BF16)
    g_ari = nc.dram_tensor("g_ari", [1, 8], F32)
    g_scr = nc.dram_tensor("g_scr", [128, 4], F32)
    g_aro = nc.dram_tensor("g_aro", [8, 8], F32)
    wqi = [nc.dram_tensor("w1qi", [256, DFF], BF16),
           nc.dram_tensor("w2qi", [256, DFF], BF16),
           nc.dram_tensor("w3qi", [704, D], BF16)]
    wqo = [nc.dram_tensor("w1qo", [D, DFF], BF16),
           nc.dram_tensor("w2qo", [D, DFF], BF16),
           nc.dram_tensor("w3qo", [DFF, D], BF16)]

    RG_PAIR = [[0, 1], [2, 3], [4, 5], [6, 7]]
    RG_WORLD = [list(range(8))]

    st_ctx = ExitStack()
    with st_ctx as ctx:
        sbt = lambda n, s, d: ctx.enter_context(nc.sbuf_tensor(n, s, d))
        sem = lambda n: Ctr(ctx.enter_context(nc.semaphore(n)))

        xt = sbt("xt_sb", [128, DCH * TLOC], F32)
        mask_sb = sbt("mask_sb", [128, 512], BF16)
        ones_bf = sbt("ones_bf", [128, 1], BF16)
        onesr_bf = sbt("onesr_bf", [1, 128], BF16)
        onesr_f = sbt("onesr_f", [1, 128], F32)
        bm20 = sbt("bm20", [128, 1], F32)
        epsln = sbt("epsln", [1, 1], F32)
        thrv = sbt("thrv", [128, 8], F32)
        gsb = sbt("gsb", [1, 8], F32)
        gred = sbt("gred", [1, 512], F32)
        echo = sbt("echo", [128, 16], BF16)
        wsum = sbt("wsum", [128, 4], F32)
        wred = sbt("wred", [128, 1], F32)
        tok0_sb = sbt("tok0_sb", [1, 1], U32)
        m_t = sbt("m_t", [1, 512], F32)
        q_t = sbt("q_t", [1, 512], F32)
        rs_t = sbt("rs_t", [1, 512], F32)
        mr_t = sbt("mr_t", [1, 512], F32)
        sqr = sbt("sqr", [128, 2 * 512], BF16)
        tmpr = sbt("tmpr", [128, 512], F32)
        lnr = sbt("lnr", [128, 4 * 512], BF16)
        qst = sbt("qst", [128, 2 * 256], F32)
        qmp = sbt("qmp", [128, 256], BF16)
        qmn = sbt("qmn", [128, 256], BF16)
        qob = sbt("qob", [128, 2 * 256], BF16)
        rr = sbt("rr", [1, 512], BF16)
        ctxf = sbt("ctxf", [128, 256], F32)

        s_in = sem("s_in")
        s_gp = sem("s_gp")
        s_cc = sem("s_cc")
        s_pe = sem("s_pe")
        s_v = sem("s_v")
        s_sc = sem("s_sc")
        s_bar = sem("s_bar")
        s_qc = sem("s_qc")
        s_sd = sem("s_sd")
        s_qt = [sem("s_qt0"), sem("s_qt1")]
        s_qb = [sem("s_qb0"), sem("s_qb1")]
        s_fw = [sem("s_fw0"), sem("s_fw1")]
        s_f3 = [sem("s_f30"), sem("s_f31")]

        blk = ctx.enter_context(nc.Block())

        @blk.vector
        def _(v):
            v.memset(ones_bf[:], 1.0)
            v.memset(onesr_bf[:], 1.0)
            v.memset(onesr_f[:], 1.0)
            v.memset(bm20[:], -CEXP)
            v.memset(epsln[:], 1e-5)
            v.memset(wsum[:], 0.0)
            v.memset(gsb[:], 0.0).then_inc(s_v.h, 1)
        consts_tgt = s_v.inc()

        @blk.sync
        def _(s):
            s.dma_start(out=tok0_sb[:], in_=tok0_d).then_inc(s_in.h, 16)
            s.dma_start(out=mask_sb[:], in_=mask_d).then_inc(s_in.h, 16)
            for c in range(DCH):
                s.dma_start(out=xt[:, c * TLOC:(c + 1) * TLOC],
                            in_=xt_d[c * 128:(c + 1) * 128, :]).then_inc(s_in.h, 16)
        tok_tgt = s_in.inc(16)
        mask_tgt = s_in.inc(16)
        xt_tgt = s_in.inc(16 * DCH)


        def barrier():
            for e in ENGS:
                @getattr(blk, e)
                def _(eng):
                    eng.nop().then_inc(s_bar.h, 1)
            tgt = s_bar.inc(5)
            for e in ENGS:
                @getattr(blk, e)
                def _(eng):
                    eng.wait_ge(s_bar.h, tgt)
            return tgt

        yago_read = [0]
        cago_read = [0]
        wag_tgts = []
        quant_emitted = [False]
        qc_used = [0, 0]
        allq_tgt = [0]
        quant_thr_tgt = [0]
        abs_state = [0, 0]

        def emit_ln(xb_scr, psS0, psS1, psB0, psB1, pre_v, to_agi, own_dst,
                    agi_pre_cc=0):
            @blk.vector
            def _(v):
                for (h, t) in pre_v:
                    v.wait_ge(h, t)
                v.tensor_copy(xb_scr[:, 0:DCH * TLOC], xt[:]).then_inc(s_v.h, 1)
            cast_tgt = s_v.inc()

            dma_tgts = [0] * (2 * DCH)
            apply_last = 0
            stats_read = 0
            sq_pe = {}
            for tt in range(2):
                for c in range(DCH):
                    i = tt * DCH + c
                    slot = i % 2
                    xb_sl = xb_scr[:, c * TLOC + tt * 512: c * TLOC + tt * 512 + 512]

                    @blk.scalar
                    def _(s):
                        if i == 0:
                            s.wait_ge(s_v.h, cast_tgt)
                        if i >= 2:
                            s.wait_ge(s_pe.h, sq_pe[i - 2])
                        s.activation(sqr[:, slot * 512:slot * 512 + 512], xb_sl,
                                     AF.Square).then_inc(s_sc.h, 1)
                    sc_t = s_sc.inc()

                    @blk.tensor
                    def _(t):
                        if i == 0:
                            t.wait_ge(s_v.h, cast_tgt)
                        if c == 0 and stats_read:
                            t.wait_ge(s_v.h, stats_read)
                        t.matmul(psS0[0:1, 0:512], ones_bf[:], xb_sl,
                                 start=(c == 0), stop=(c == DCH - 1))
                        t.wait_ge(s_sc.h, sc_t)
                        mm = t.matmul(psS1[0:1, 0:512], ones_bf[:],
                                      sqr[:, slot * 512:slot * 512 + 512],
                                      start=(c == 0), stop=(c == DCH - 1))
                        mm.then_inc(s_pe.h, 1)
                    sq_pe[i] = s_pe.inc()
                sums_done = sq_pe[tt * DCH + DCH - 1]

                @blk.vector
                def _(v):
                    v.wait_ge(s_pe.h, sums_done)
                    v.tensor_scalar(m_t[0:1, :], psS0[0:1, 0:512], 1.0 / D, None,
                                    AL.mult)
                    v.tensor_scalar(q_t[0:1, :], psS1[0:1, 0:512], 1.0 / D, None,
                                    AL.mult).then_inc(s_v.h, 1)
                    v.wait_ge(s_v.h, s_v.inc())
                    v.tensor_tensor(mr_t[0:1, :], m_t[0:1, :], m_t[0:1, :],
                                    AL.mult).then_inc(s_v.h, 1)
                    v.wait_ge(s_v.h, s_v.inc())
                    v.tensor_tensor(q_t[0:1, :], q_t[0:1, :], mr_t[0:1, :],
                                    AL.subtract).then_inc(s_v.h, 1)
                stats_read = s_v.inc()

                @blk.scalar
                def _(s):
                    s.wait_ge(s_v.h, stats_read)
                    s.activation(mr_t[0:1, :], q_t[0:1, :], AF.Ln,
                                 bias=epsln[0:1, 0:1])
                    s.activation(rs_t[0:1, :], mr_t[0:1, :], AF.Exp,
                                 scale=-0.5).then_inc(s_sc.h, 1)
                rstd_tgt = s_sc.inc()

                @blk.vector
                def _(v):
                    v.wait_ge(s_sc.h, rstd_tgt)
                    v.scalar_tensor_tensor(mr_t[0:1, :], m_t[0:1, :], -1.0,
                                           rs_t[0:1, :], AL.mult,
                                           AL.mult).then_inc(s_v.h, 1)
                mr_tgt = s_v.inc()

                @blk.tensor
                def _(t):
                    t.wait_ge(s_v.h, mr_tgt)
                    if apply_last:
                        t.wait_ge(s_v.h, apply_last)
                    t.matmul(psB0[:, 0:512], onesr_f[:], rs_t[0:1, :],
                             start=True, stop=True)
                    t.matmul(psB1[:, 0:512], onesr_f[:], mr_t[0:1, :],
                             start=True, stop=True).then_inc(s_pe.h, 1)
                bc_tgt = s_pe.inc()

                for c in range(DCH):
                    i = tt * DCH + c
                    slot = i % 4
                    xt_sl = xt[:, c * TLOC + tt * 512: c * TLOC + tt * 512 + 512]

                    @blk.vector
                    def _(v):
                        if c == 0:
                            v.wait_ge(s_pe.h, bc_tgt)
                        if to_agi and i >= 4:
                            v.wait_ge(s_gp.h, dma_tgts[i - 4])
                        v.tensor_tensor(tmpr[:, 0:512],
                                        xt_sl, psB0[:, 0:512], AL.mult)
                        if to_agi:
                            dst = lnr[:, slot * 512:slot * 512 + 512]
                        else:
                            dst = own_dst[:, c * TLOC + tt * 512:
                                          c * TLOC + tt * 512 + 512]
                        v.tensor_tensor(dst, tmpr[:, 0:512],
                                        psB1[:, 0:512], AL.add).then_inc(s_v.h, 1)
                    a_t = s_v.inc()
                    apply_last = a_t

                    if to_agi:
                        @blk.gpsimd
                        def _(g):
                            g.wait_ge(s_v.h, a_t)
                            if i == 0 and agi_pre_cc:
                                g.wait_ge(s_cc.h, agi_pre_cc)
                            g.dma_start(out=y_agi.ap()[c * 128:(c + 1) * 128,
                                                       tt * 512:tt * 512 + 512],
                                        in_=lnr[:, slot * 512:slot * 512 + 512]
                                        ).then_inc(s_gp.h, 16)
                        dma_tgts[i] = s_gp.inc(16)
            return apply_last, (dma_tgts[-1] if to_agi else 0), stats_read

        def q_tiles():
            tiles = []
            for m in range(3):
                if m < 2:
                    rows, cols = 256, DFF
                else:
                    rows, cols = 704, D
                r0 = 0
                while r0 < rows:
                    pr = min(128, rows - r0)
                    for c0 in range(0, cols, 256):
                        tiles.append((m, r0, pr, c0))
                    r0 += pr
            return tiles

        def emit_quant_abs(stats_free_tgt):
            tiles = q_tiles()
            vtail = [0, 0]
            for i, (m, r0, pr, c0) in enumerate(tiles):
                par = i % 2

                @blk.sync
                def _(s):
                    if vtail[par]:
                        s.wait_ge(s_v.h, vtail[par])
                    s.dma_start(out=qst[0:pr, par * 256:par * 256 + 256],
                                in_=w_sl_d[m][r0:r0 + pr, c0:c0 + 256]
                                ).then_inc(s_qt[par].h, 16)
                d_t = s_qt[par].inc(16)

                @blk.gpsimd
                def _(g):
                    g.wait_ge(s_qt[par].h, d_t)
                    g.tensor_copy(echo[0:pr, 0:16],
                                  qst[0:pr, par * 256 + 240:par * 256 + 256]
                                  ).then_inc(s_qc.h, 1)
                e_t = s_qc.inc()

                @blk.vector
                def _(v):
                    v.wait_ge(s_qc.h, e_t)
                    v.tensor_reduce(wred[0:pr, 0:1],
                                    qst[0:pr, par * 256:par * 256 + 256],
                                    AX.X, AL.add,
                                    apply_absolute_value=True).then_inc(s_v.h, 1)
                red_t = s_v.inc()

                @blk.vector
                def _(v):
                    v.wait_ge(s_v.h, red_t)
                    v.tensor_tensor(wsum[0:pr, m:m + 1], wsum[0:pr, m:m + 1],
                                    wred[0:pr, 0:1], AL.add).then_inc(s_v.h, 1)
                vtail[par] = s_v.inc()
            abs_done = vtail[(len(tiles) - 1) % 2]

            @blk.gpsimd
            def _(g):
                g.wait_ge(s_v.h, abs_done)
                g.dma_start(out=g_scr.ap(), in_=wsum[:, 0:4]).then_inc(s_gp.h, 16)
                if dbg:
                    g.dma_start(out=t_ws_d, in_=wsum[:, 0:4]).then_inc(s_gp.h, 16)
            g_s1 = s_gp.inc(16)
            if dbg:
                s_gp.inc(16)

            @blk.gpsimd
            def _(g):
                g.wait_ge(s_gp.h, g_s1)
                g.dma_start(out=gred[0:1, :],
                            in_=g_scr.ap().rearrange("r m -> (r m)")[None, :]
                            ).then_inc(s_gp.h, 16)
            g_s2 = s_gp.inc(16)

            @blk.gpsimd
            def _(g):
                g.wait_ge(s_gp.h, g_s2)
                w = 256
                while w >= 4:
                    g.tensor_tensor(gred[0:1, 0:w], gred[0:1, 0:w],
                                    gred[0:1, w:2 * w],
                                    AL.add).then_inc(s_qc.h, 1)
                    g.wait_ge(s_qc.h, s_qc.inc())
                    w //= 2
                g.tensor_copy(gsb[0:1, 0:4], gred[0:1, 0:4]).then_inc(s_qc.h, 1)
            g_v = s_qc.inc()

            @blk.gpsimd
            def _(g):
                g.wait_ge(s_qc.h, g_v)
                g.dma_start(out=g_ari.ap(), in_=gsb[0:1, 0:8]).then_inc(s_gp.h, 16)
            g_d = s_gp.inc(16)

            @blk.gpsimd
            def _(g):
                g.wait_ge(s_gp.h, g_d)
                g.collective_compute("AllGather", AL.bypass, replica_groups=RG_WORLD,
                                     ins=[g_ari.ap().opt()],
                                     outs=[g_aro.ap().opt()]).then_inc(s_cc.h, 1)
            ar_tgt = s_cc.inc()

            @blk.gpsimd
            def _(g):
                g.wait_ge(s_cc.h, ar_tgt)
                g.dma_start(out=gred[0:1, 0:64],
                            in_=g_aro.ap().rearrange("r m -> (r m)")[None, :]
                            ).then_inc(s_gp.h, 16)
            gg_t = s_gp.inc(16)

            @blk.gpsimd
            def _(g):
                g.wait_ge(s_gp.h, gg_t)
                g.tensor_copy(echo[0:1, 0:16], gred[0:1, 48:64]).then_inc(s_qc.h, 1)
                g.wait_ge(s_qc.h, s_qc.inc())
                w = 32
                while w >= 8:
                    g.tensor_tensor(gred[0:1, 0:w], gred[0:1, 0:w],
                                    gred[0:1, w:2 * w],
                                    AL.add).then_inc(s_qc.h, 1)
                    g.wait_ge(s_qc.h, s_qc.inc())
                    w //= 2
                g.tensor_copy(gsb[0:1, 0:8], gred[0:1, 0:8]).then_inc(s_qc.h, 1)
            gf_t = s_qc.inc()

            @blk.gpsimd
            def _(g):
                g.wait_ge(s_qc.h, gf_t)
                g.nop().then_inc(s_gp.h, 16)
            return s_gp.inc(16), vtail[(len(tiles) - 1) % 2]

        q_dtail = [0, 0]

        def emit_quant_thr(gout_tgt):
            @blk.gpsimd
            def _(g):
                g.wait_ge(s_gp.h, gout_tgt)
                g.tensor_scalar(gsb[0:1, 0:4], gsb[0:1, 0:4], 1.0 / WCNT, None,
                                AL.mult)
                g.tensor_scalar(gsb[0:1, 0:4], gsb[0:1, 0:4], 1e-8, None, AL.add)
                g.tensor_scalar(gsb[0:1, 0:4], gsb[0:1, 0:4], 0.5, None, AL.mult)
                g.tensor_scalar(gsb[0:1, 4:8], gsb[0:1, 0:4], -1.0, None,
                                AL.mult).then_inc(s_qc.h, 1)
            thr_c = s_qc.inc()

            @blk.gpsimd
            def _(g):
                g.wait_ge(s_qc.h, thr_c)
                g.dma_start(out=g_ari.ap(), in_=gsb[0:1, 0:8]).then_inc(s_gp.h, 16)
            thr_d = s_gp.inc(16)

            @blk.gpsimd
            def _(g):
                g.wait_ge(s_gp.h, thr_d)
                g.dma_start(out=thrv[:, 0:8],
                            in_=_bcast_ap(g_ari.ap())).then_inc(s_gp.h, 16)
            quant_thr_tgt[0] = s_gp.inc(16)

        def emit_quant_tiles(lo, hi, abs_vtail):
            tiles = q_tiles()
            for i in range(lo, min(hi, len(tiles))):
                m, r0, pr, c0 = tiles[i]
                par = i % 2

                @blk.gpsimd
                def _(g):
                    if i == 0:
                        g.wait_ge(s_v.h, abs_vtail)
                        g.wait_ge(s_gp.h, quant_thr_tgt[0])
                    if i >= 2:
                        g.wait_ge(s_qc.h, qc_used[par])
                    g.dma_start(out=qst[0:pr, par * 256:par * 256 + 256],
                                in_=w_sl_d[m][r0:r0 + pr, c0:c0 + 256]
                                ).then_inc(s_qt[par].h, 16)
                d_t = s_qt[par].inc(16)

                @blk.gpsimd
                def _(g):
                    g.wait_ge(s_qt[par].h, d_t)
                    if q_dtail[par]:
                        g.wait_ge(s_qb[par].h, q_dtail[par])
                    g.tensor_copy(echo[0:pr, 0:16],
                                  qst[0:pr, par * 256 + 240:par * 256 + 256]
                                  ).then_inc(s_qc.h, 1)
                    g.wait_ge(s_qc.h, s_qc.inc())
                    g.tensor_scalar(qmp[0:pr, 0:256],
                                    qst[0:pr, par * 256:par * 256 + 256],
                                    thrv[0:pr, m:m + 1], None, AL.is_ge)
                    g.tensor_scalar(qmn[0:pr, 0:256],
                                    qst[0:pr, par * 256:par * 256 + 256],
                                    thrv[0:pr, 4 + m:5 + m], None,
                                    AL.is_le).then_inc(s_qc.h, 1)
                    g.wait_ge(s_qc.h, s_qc.inc())
                    g.tensor_tensor(qob[0:pr, par * 256:par * 256 + 256],
                                    qmp[0:pr, 0:256], qmn[0:pr, 0:256],
                                    AL.subtract).then_inc(s_qc.h, 1)
                qc_used[par] = s_qc.inc()

                @blk.gpsimd
                def _(g):
                    g.wait_ge(s_qc.h, qc_used[par])
                    g.dma_start(out=wqi[m].ap()[r0:r0 + pr, c0:c0 + 256],
                                in_=qob[0:pr, par * 256:par * 256 + 256]
                                ).then_inc(s_qb[par].h, 16)
                q_dtail[par] = s_qb[par].inc(16)
            if hi >= len(tiles):
                allq_tgt[0] = (s_qb[0].v, s_qb[1].v)

        def emit_wag(m):
            @blk.gpsimd
            def _(g):
                g.wait_ge(s_qb[0].h, allq_tgt[0][0])
                g.wait_ge(s_qb[1].h, allq_tgt[0][1])
                g.collective_compute("AllGather", AL.bypass,
                                     replica_groups=RG_WORLD,
                                     ins=[wqi[m].ap().opt()],
                                     outs=[wqo[m].ap().opt()]
                                     ).then_inc(s_cc.h, 1)
            wag_tgts.append(s_cc.inc())

        def emit_attn(kind, sub_idx, pre_v):
            sfx = f"_s{sub_idx}"
            with ExitStack() as actx:
                asb = lambda n, s, d: actx.enter_context(
                    nc.sbuf_tensor(n + sfx, s, d))
                qh = asb("qh", [128, TGLB], BF16)
                kh = asb("kh", [128, TGLB], BF16)
                vh = asb("vh", [128, DCH * VW], BF16)
                wslb = [asb("wslb0", [128, DCH * 448], BF16),
                        asb("wslb1", [128, DCH * 448], BF16)]
                p_sb = asb("p_sb", [128, 4 * 512], BF16)
                ctxs = asb("ctxs", [128, TGLB], BF16)

                scale = SCA if kind == "a" else SCB
                nunits = NHA if kind == "a" else NGR
                if dbg_units:
                    nunits = min(nunits, dbg_units)
                res_last = 0

                with ExitStack() as yctx:
                    ysb = lambda n, s, d: yctx.enter_context(
                        nc.sbuf_tensor(n + sfx, s, d))
                    yps = lambda n, s: yctx.enter_context(
                        nc.psum_tensor(n + sfx, s, F32))
                    yg = ysb("yg", [128, DCH * TGLB], BF16)

                    with ExitStack() as lctx:
                        lps = lambda n, s: lctx.enter_context(
                            nc.psum_tensor(n + sfx, s, F32))
                        psS0 = lps("psS0", [1, 512])
                        psS1 = lps("psS1", [1, 512])
                        psB0 = lps("psB0", [128, 512])
                        psB1 = lps("psB1", [128, 512])
                        apply_last, ln_dma, stats_free = emit_ln(
                            yg, psS0, psS1, psB0, psB1, pre_v,
                            to_agi=True, own_dst=None,
                            agi_pre_cc=(yag1_tgt[0] if sub_idx == 2 else 0))

                        @blk.gpsimd
                        def _(g):
                            g.wait_ge(s_gp.h, ln_dma)
                            if sub_idx == 2:
                                g.wait_ge(s_in.h, yago_read[0])
                            g.collective_compute(
                                "AllGather", AL.bypass, replica_groups=RG_PAIR,
                                ins=[y_agi.ap().opt()],
                                outs=[y_ago.ap().opt()]).then_inc(s_cc.h, 1)
                        yag = s_cc.inc()
                        if sub_idx == 0:
                            yag1_tgt[0] = yag

                        if sub_idx == 0 and not quant_emitted[0] and not no_quant:
                            abs_state[0], abs_state[1] = emit_quant_abs(
                                stats_free)

                    @blk.sync
                    def _(s):
                        s.wait_ge(s_cc.h, yag)
                        for hh in range(2):
                            for c in range(DCH):
                                s.dma_start(
                                    out=yg[:, c * TGLB + hh * TLOC:
                                           c * TGLB + hh * TLOC + TLOC],
                                    in_=y_ago.ap()[hh * D + c * 128:
                                                   hh * D + c * 128 + 128, :]
                                    ).then_inc(s_in.h, 16)
                    yg_tgt = s_in.inc(16 * 2 * DCH)
                    yago_read[0] = yg_tgt

                    psP = [yps("psP0", [128, 512]), yps("psP1", [128, 512])]
                    if kind == "a":
                        psS = [yps("psSc0", [128, 512]), yps("psSc1", [128, 512])]
                        psC = [yps("psC0", [128, 512]), yps("psC1", [128, 512])]
                        psD = [yps("psD0", [1, 512]), yps("psD1", [1, 512])]
                    else:
                        psS = [yps("psSc0", [128, 512]), yps("psSc1", [128, 512]),
                               yps("psSc2", [128, 512]), yps("psSc3", [128, 512])]
                        psC = [yps("psC0", [128, 512]), yps("psC1", [128, 512])]

                    if kind == "b":
                        @blk.vector
                        def _(v):
                            vv = vh[:].rearrange("p (tc w) -> p tc w", w=VW)
                            v.memset(vv[:, :, 64:65], 1.0).then_inc(s_v.h, 1)
                        s_v.inc()

                    pp = [0, 0]
                    slab_free = [0, 0]
                    stail = [0, 0]
                    av_tail = 0
                    ctx_dma = 0
                    cago_dmas = []
                    sc_free = {}
                    p_free = {}
                    cb_free = [0, 0]
                    db_free = [0, 0]
                    norm_sc = [0, 0]
                    bc_prev = [0]

                    def emit_proj(par, col, width, dst_sb, n_out, first_waits,
                                  v_first_waits):
                        tail = 0
                        pe_t = 0
                        for tt4 in range(n_out):
                            bp = tt4 % 2

                            @blk.tensor
                            def _(t):
                                if tt4 == 0:
                                    for (h, tg) in first_waits:
                                        t.wait_ge(h, tg)
                                if pp[bp]:
                                    t.wait_ge(s_v.h, pp[bp])
                                last = None
                                for c in range(DCH):
                                    last = t.matmul(
                                        psP[bp][:, 0:512],
                                        wslb[par][:, c * 448 + col:
                                                  c * 448 + col + width],
                                        yg[:, c * TGLB + tt4 * 512:
                                           c * TGLB + tt4 * 512 + 512],
                                        start=(c == 0), stop=(c == DCH - 1))
                                last.then_inc(s_pe.h, 1)
                            pe_t = s_pe.inc()

                            @blk.vector
                            def _(v):
                                v.wait_ge(s_pe.h, pe_t)
                                if tt4 == 0:
                                    for (h, tg) in v_first_waits:
                                        v.wait_ge(h, tg)
                                v.tensor_copy(dst_sb[:, tt4 * 512:tt4 * 512 + 512],
                                              psP[bp][:, 0:512]).then_inc(s_v.h, 1)
                            tail = s_v.inc()
                            pp[bp] = tail
                        return tail, pe_t

                    def emit_vproj(par, col, vwid, first_waits, v_first_waits):
                        tail = 0
                        for tc in range(DCH):
                            bp = tc % 2

                            @blk.tensor
                            def _(t):
                                if tc == 0:
                                    for (h, tg) in first_waits:
                                        t.wait_ge(h, tg)
                                if pp[bp]:
                                    t.wait_ge(s_v.h, pp[bp])
                                last = None
                                for c in range(DCH):
                                    last = t.matmul(
                                        psP[bp][:, 0:vwid],
                                        yg[:, c * TGLB + tc * 128:
                                           c * TGLB + tc * 128 + 128],
                                        wslb[par][:, c * 448 + col:
                                                  c * 448 + col + vwid],
                                        start=(c == 0), stop=(c == DCH - 1))
                                last.then_inc(s_pe.h, 1)
                            pe_t = s_pe.inc()

                            @blk.vector
                            def _(v):
                                v.wait_ge(s_pe.h, pe_t)
                                if tc == 0:
                                    for (h, tg) in v_first_waits:
                                        v.wait_ge(h, tg)
                                v.tensor_copy(vh[:, tc * VW:tc * VW + vwid],
                                              psP[bp][:, 0:vwid]).then_inc(s_v.h, 1)
                            tail = s_v.inc()
                            pp[bp] = tail
                        return tail

                    if dbg and sub_idx == 0:
                        @blk.gpsimd
                        def _(g):
                            g.wait_ge(s_in.h, yg_tgt)
                            g.dma_start(out=t_yg_d, in_=yg[:]).then_inc(s_gp.h, 16)
                        s_gp.inc(16)

                    def emit_slab(uu):
                        upar = uu % 2
                        if kind == "a":
                            usrcs = [(wpq_d, uu * 128, 128),
                                     (wpk_d, uu * 128, 128),
                                     (wpv_d, uu * 128, 128)]
                        else:
                            usrcs = [(wqb_d, uu * 256, 256),
                                     (wkb_d, uu * 128, 128),
                                     (wvb_d, uu * 64, 64)]

                        @blk.gpsimd
                        def _(g):
                            if slab_free[upar]:
                                g.wait_ge(s_pe.h, slab_free[upar])
                            wv = wslb[upar][:].rearrange("p (c q) -> p c q", q=448)
                            off = 0
                            for (dap, col0, w) in usrcs:
                                g.dma_start(
                                    out=wv[:, :, off:off + w],
                                    in_=dap.rearrange("(c p) o -> p c o", p=128)
                                    [:, :, col0:col0 + w]).then_inc(s_gp.h, 16)
                                off += w
                        dma_t = s_gp.inc(16 * len(usrcs))

                        @blk.gpsimd
                        def _(g):
                            g.wait_ge(s_gp.h, dma_t)
                            sw = sum(w for (_, _, w) in usrcs)
                            g.tensor_copy(echo[:, 0:16],
                                          wslb[upar][:, (DCH - 1) * 448 + sw - 16:
                                                     (DCH - 1) * 448 + sw]
                                          ).then_inc(s_qc.h, 1)
                        return s_qc.inc()

                    cast_tgts = {0: emit_slab(0)}
                    n_qt = len(q_tiles())
                    qchunk = (n_qt + nunits - 2) // max(nunits - 1, 1)

                    for u in range(nunits):
                        par = u % 2
                        if kind == "a":
                            kcol, vcol, vwid = 128, 256, 128
                            njp = 1
                        else:
                            kcol, vcol, vwid = 256, 384, 64
                            njp = 2

                        if u + 1 < nunits:
                            cast_tgts[u + 1] = emit_slab(u + 1)
                        if sub_idx == 0 and not no_quant:
                            if u == 0:
                                emit_quant_thr(abs_state[0])
                            if u >= 1:
                                emit_quant_tiles((u - 1) * qchunk, u * qchunk,
                                                 abs_state[1])
                            if u == nunits - 1:
                                emit_quant_tiles((u) * qchunk, n_qt,
                                                 abs_state[1])
                                emit_wag(0)

                        fw = [(s_qc.h, cast_tgts[u]), (s_in.h, yg_tgt)]
                        vw = [(s_pe.h, av_tail)] if av_tail else []
                        kh_tail, _ = emit_proj(par, kcol, 128, kh, 4, fw, vw)
                        if dbg and sub_idx == 0 and u == 0:
                            @blk.gpsimd
                            def _(g):
                                g.wait_ge(s_v.h, kh_tail)
                                g.dma_start(out=t_kh2_d, in_=kh[:]).then_inc(
                                    s_gp.h, 16)
                            s_gp.inc(16)
                        vh_tail = emit_vproj(par, vcol, vwid, [], vw)

                        q_pe_last = 0
                        for jp in range(njp):
                            qcol = 0 if kind == "a" else jp * 128
                            vwq = [(s_pe.h, av_tail)] if av_tail else []
                            qh_tail, q_pe_last = emit_proj(par, qcol, 128, qh, 4,
                                                           [], vwq)
                            last_avs = {}
                            qkv_wait_done = False

                            for st in range(NST):
                                npairs = st + 1
                                stp = st % 2
                                for pb in range(npairs):
                                    b0, b1 = 2 * pb, 2 * pb + 1
                                    spar = pb % 2
                                    diag = (pb == npairs - 1)

                                    @blk.tensor
                                    def _(t):
                                        if not qkv_wait_done:
                                            t.wait_ge(s_v.h, qh_tail)
                                            t.wait_ge(s_v.h, kh_tail)
                                            t.wait_ge(s_v.h, vh_tail)
                                        for sj in range(2 if kind == "b" else 1):
                                            sidx = (2 * spar + sj) if kind == "b" \
                                                else spar
                                            if sc_free.get(sidx):
                                                t.wait_ge(s_sc.h, sc_free[sidx])
                                            if kind == "a":
                                                t.matmul(psS[sidx][:, 0:256],
                                                         kh[:, b0 * 128:b0 * 128 + 128],
                                                         qh[:, st * 256:st * 256 + 256],
                                                         start=True, stop=False)
                                                mm = t.matmul(
                                                    psS[sidx][:, 256:512],
                                                    kh[:, b1 * 128:b1 * 128 + 128],
                                                    qh[:, st * 256:st * 256 + 256],
                                                    start=False, stop=True)
                                            else:
                                                p0 = sj * 64
                                                tp = (p0, 0)
                                                t.matmul(psS[sidx][:, 0:256],
                                                         kh[p0:p0 + 64,
                                                            b0 * 128:b0 * 128 + 128],
                                                         qh[p0:p0 + 64,
                                                            st * 256:st * 256 + 256],
                                                         start=True, stop=False,
                                                         tile_position=tp)
                                                mm = t.matmul(
                                                    psS[sidx][:, 256:512],
                                                    kh[p0:p0 + 64,
                                                       b1 * 128:b1 * 128 + 128],
                                                    qh[p0:p0 + 64,
                                                       st * 256:st * 256 + 256],
                                                    start=False, stop=True,
                                                    tile_position=tp)
                                            mm.then_inc(s_pe.h, 1)
                                    qkv_wait_done = True
                                    qk_t = []
                                    for sj in range(2 if kind == "b" else 1):
                                        qk_t.append(s_pe.inc())

                                    av_waits = []
                                    for sj in range(2 if kind == "b" else 1):
                                        sidx = (2 * spar + sj) if kind == "b" \
                                            else spar
                                        slot = sidx if kind == "b" else spar

                                        @blk.scalar
                                        def _(s):
                                            s.wait_ge(s_pe.h, qk_t[sj])
                                            if p_free.get(slot):
                                                s.wait_ge(s_pe.h, p_free[slot])
                                            s.activation(
                                                p_sb[:, slot * 512:slot * 512 + 512],
                                                psS[sidx][:, 0:512], AF.Exp,
                                                bias=bm20[:, 0:1],
                                                scale=scale).then_inc(s_sc.h, 1)
                                        e_t = s_sc.inc()
                                        sc_free[sidx] = e_t
                                        if diag:
                                            @blk.vector
                                            def _(v):
                                                v.wait_ge(s_sc.h, e_t)
                                                v.tensor_tensor(
                                                    p_sb[:, slot * 512:
                                                         slot * 512 + 512],
                                                    p_sb[:, slot * 512:
                                                         slot * 512 + 512],
                                                    mask_sb[:],
                                                    AL.mult).then_inc(s_v.h, 1)
                                            av_waits.append((s_v.h, s_v.inc()))
                                        else:
                                            av_waits.append((s_sc.h, e_t))

                                    @blk.tensor
                                    def _(t):
                                        for (h, tg) in av_waits:
                                            t.wait_ge(h, tg)
                                        if pb == 0:
                                            if cb_free[stp]:
                                                t.wait_ge(s_v.h, cb_free[stp])
                                            if kind == "a" and db_free[stp]:
                                                t.wait_ge(s_v.h, db_free[stp])
                                        last = None
                                        for sj in range(2 if kind == "b" else 1):
                                            slot = (2 * spar + sj) if kind == "b" \
                                                else spar
                                            co = sj * 256
                                            vb = 65 if kind == "b" else 128
                                            first = (pb == 0 and sj == 0)
                                            for (bb, po) in ((b0, 0), (b1, 256)):
                                                last = t.matmul(
                                                    psC[stp][0:65, co:co + 256]
                                                    if kind == "b" else
                                                    psC[stp][:, 0:256],
                                                    vh[:, bb * VW:bb * VW + vb],
                                                    p_sb[:, slot * 512 + po:
                                                         slot * 512 + po + 256],
                                                    start=(first and bb == b0),
                                                    stop=(diag and bb == b1))
                                        if kind == "a":
                                            for (bb, po) in ((b0, 0), (b1, 256)):
                                                last = t.matmul(
                                                    psD[stp][0:1, 0:256],
                                                    ones_bf[:],
                                                    p_sb[:, spar * 512 + po:
                                                         spar * 512 + po + 256],
                                                    start=(pb == 0 and bb == b0),
                                                    stop=(diag and bb == b1))
                                        last.then_inc(s_pe.h, 1)
                                    av_t = s_pe.inc()
                                    for sj in range(2 if kind == "b" else 1):
                                        slot = (2 * spar + sj) if kind == "b" \
                                            else spar
                                        p_free[slot] = av_t
                                    av_tail = av_t
                                last_avs[st] = av_tail

                                @blk.vector
                                def _(v):
                                    v.wait_ge(s_pe.h, last_avs[st])
                                    if bc_prev[0]:
                                        v.wait_ge(s_pe.h, bc_prev[0])
                                    with nc.allow_low_precision(
                                            reason="softmax 1/sum in bf16 for "
                                                   "the broadcast matmul"):
                                        if kind == "a":
                                            v.reciprocal(rr[0:1, 0:256],
                                                         psD[stp][0:1, 0:256]
                                                         ).then_inc(s_v.h, 1)
                                        else:
                                            v.reciprocal(rr[0:1, 0:256],
                                                         psC[stp][64:65, 0:256])
                                            v.reciprocal(rr[0:1, 256:512],
                                                         psC[stp][64:65, 256:512]
                                                         ).then_inc(s_v.h, 1)
                                rr_t = s_v.inc()
                                if kind == "a":
                                    db_free[stp] = rr_t

                                @blk.scalar
                                def _(s):
                                    s.wait_ge(s_pe.h, last_avs[st])
                                    if norm_sc[1 - stp]:
                                        s.wait_ge(s_v.h, norm_sc[1 - stp])
                                    if kind == "a":
                                        s.activation(ctxf[:, 0:256],
                                                     psC[stp][:, 0:256],
                                                     AF.Copy).then_inc(s_sc.h, 1)
                                    else:
                                        s.activation(ctxf[0:64, 0:256],
                                                     psC[stp][0:64, 0:256],
                                                     AF.Copy)
                                        s.activation(ctxf[64:128, 0:256],
                                                     psC[stp][0:64, 256:512],
                                                     AF.Copy).then_inc(s_sc.h, 1)
                                cf_t = s_sc.inc()

                                @blk.tensor
                                def _(t):
                                    t.wait_ge(s_v.h, rr_t)
                                    t.wait_ge(s_sc.h, cf_t)
                                    if kind == "a":
                                        t.matmul(psC[stp][:, 0:256], onesr_bf[:],
                                                 rr[0:1, 0:256], start=True,
                                                 stop=True).then_inc(s_pe.h, 1)
                                    else:
                                        t.matmul(psC[stp][:, 0:512], onesr_bf[:],
                                                 rr[0:1, 0:512], start=True,
                                                 stop=True).then_inc(s_pe.h, 1)
                                bc_t = s_pe.inc()
                                bc_prev[0] = bc_t

                                @blk.vector
                                def _(v):
                                    v.wait_ge(s_pe.h, bc_t)
                                    if st == 0 and ctx_dma:
                                        v.wait_ge(s_gp.h, ctx_dma)
                                    if kind == "a":
                                        v.tensor_tensor(
                                            ctxs[:, st * 256:st * 256 + 256],
                                            ctxf[:, 0:256], psC[stp][:, 0:256],
                                            AL.mult).then_inc(s_v.h, 1)
                                    else:
                                        v.tensor_tensor(
                                            ctxs[0:64, st * 256:st * 256 + 256],
                                            ctxf[0:64, 0:256],
                                            psC[stp][0:64, 0:256], AL.mult)
                                        v.tensor_tensor(
                                            ctxs[64:128, st * 256:st * 256 + 256],
                                            ctxf[64:128, 0:256],
                                            psC[stp][64:128, 256:512],
                                            AL.mult).then_inc(s_v.h, 1)
                                n_t = s_v.inc()
                                cb_free[stp] = n_t
                                norm_sc[stp] = n_t

                            norm_all = norm_sc[(NST - 1) % 2]
                            row0 = u * 128 if kind == "a" else (u * 4 + jp * 2) * 64

                            @blk.gpsimd
                            def _(g):
                                g.wait_ge(s_v.h, norm_all)
                                g.dma_start(out=c_agi.ap()[row0:row0 + 128, :],
                                            in_=ctxs[:]).then_inc(s_gp.h, 16)
                            ctx_dma = s_gp.inc(16)
                            cago_dmas.append(ctx_dma)
                            if dbg and sub_idx == 0 and u == 0 and jp == 0:
                                @blk.gpsimd
                                def _(g):
                                    g.dma_start(out=t_qh_d, in_=qh[:]).then_inc(
                                        s_gp.h, 16)
                                    g.dma_start(out=t_kh_d, in_=kh[:]).then_inc(
                                        s_gp.h, 16)
                                    g.dma_start(out=t_vh_d, in_=vh[:]).then_inc(
                                        s_gp.h, 16)
                                    g.dma_start(out=t_cx_d, in_=ctxs[:]).then_inc(
                                        s_gp.h, 16)
                                    g.dma_start(out=t_p_d, in_=p_sb[:]).then_inc(
                                        s_gp.h, 16)
                                s_gp.inc(16 * 5)

                        slab_free[par] = q_pe_last

                    last_ctx_dma = cago_dmas[-1]

                barrier()
                with ExitStack() as octx:
                    ops = lambda n, s: octx.enter_context(
                        nc.psum_tensor(n + sfx, s, F32))
                    ctxrhs = octx.enter_context(
                        nc.sbuf_tensor("ctxrhs" + sfx, [128, DCH * 1024], BF16))
                    psO = [ops("psO0", [128, 512]), ops("psO1", [128, 512])]

                    @blk.gpsimd
                    def _(g):
                        g.wait_ge(s_gp.h, last_ctx_dma)
                        if sub_idx == 2:
                            g.wait_ge(s_sd.h, cago_read[0])
                        if dbg and sub_idx == 0:
                            g.dma_start(out=t_ci_d, in_=c_agi.ap()).then_inc(
                                s_gp.h, 16)
                        g.collective_compute(
                            "AllGather", AL.bypass, replica_groups=RG_PAIR,
                            ins=[c_agi.ap().opt()],
                            outs=[c_ago.ap().opt()]).then_inc(s_cc.h, 1)
                    cag = s_cc.inc()
                    if dbg and sub_idx == 0:
                        s_gp.inc(16)

                    if sub_idx == 0 and len(wag_tgts) == 1 and not no_quant:
                        emit_wag(1)
                        emit_wag(2)

                    @blk.scalar
                    def _(s):
                        s.wait_ge(s_cc.h, cag)
                        s.wait_ge(s_in.h, tok_tgt)
                        reg = s.alloc_register()
                        s.reg_load(reg, tok0_sb[0:1, 0:1])
                        rvs = [RuntimeValue(reg, min_val=0, max_val=TLOC)]
                        cgr = c_ago.ap().rearrange("(cc p) t -> p cc t", p=128)
                        orr = ctxrhs[:].rearrange("p (cc t) -> p cc t", t=1024)
                        for q4 in range(4):
                            s.dma_start(
                                out=orr[:, q4 * 4:(q4 + 1) * 4, :],
                                in_=cgr[:, q4 * 4:(q4 + 1) * 4,
                                        bass.ds(rvs[0], 1024)]
                                ).then_inc(s_sd.h, 16)
                    rhs_tgt = s_sd.inc(16 * 4)
                    cago_read[0] = rhs_tgt
                    if dbg and sub_idx == 0:
                        @blk.gpsimd
                        def _(g):
                            g.wait_ge(s_sd.h, rhs_tgt)
                            g.dma_start(out=t_cr_d, in_=ctxrhs[:]).then_inc(
                                s_gp.h, 16)
                        s_gp.inc(16)

                    wo_d = woa_d if kind == "a" else wob_d
                    oslab_free = list(slab_free)
                    opp = [0, 0]
                    for oc in range(DCH):
                        par = oc % 2

                        @blk.gpsimd
                        def _(g):
                            if oslab_free[par]:
                                g.wait_ge(s_pe.h, oslab_free[par])
                            wv = wslb[par][:].rearrange("p (c q) -> p c q", q=448)
                            g.dma_start(
                                out=wv[:, :, 0:128],
                                in_=wo_d.rearrange("(c p) o -> p c o", p=128)
                                [:, :, oc * 128:(oc + 1) * 128]).then_inc(
                                    s_gp.h, 16)
                        cast_tgt = s_gp.inc(16)

                        @blk.gpsimd
                        def _(g):
                            g.wait_ge(s_gp.h, cast_tgt)
                            g.tensor_copy(echo[:, 0:16],
                                          wslb[par][:, (DCH - 1) * 448 + 112:
                                                    (DCH - 1) * 448 + 128]
                                          ).then_inc(s_qc.h, 1)
                        echo_t = s_qc.inc()

                        pe_ts = []
                        for tt in range(2):
                            bp = (2 * oc + tt) % 2

                            @blk.tensor
                            def _(t):
                                if tt == 0:
                                    t.wait_ge(s_qc.h, echo_t)
                                    if oc == 0:
                                        t.wait_ge(s_sd.h, rhs_tgt)
                                if opp[bp]:
                                    t.wait_ge(s_v.h, opp[bp])
                                last = None
                                for cc in range(DCH):
                                    last = t.matmul(
                                        psO[bp][:, 0:512],
                                        wslb[par][:, cc * 448:cc * 448 + 128],
                                        ctxrhs[:, cc * 1024 + tt * 512:
                                               cc * 1024 + tt * 512 + 512],
                                        start=(cc == 0), stop=(cc == DCH - 1))
                                last.then_inc(s_pe.h, 1)
                            pe_t = s_pe.inc()
                            pe_ts.append(pe_t)

                            @blk.vector
                            def _(v):
                                v.wait_ge(s_pe.h, pe_t)
                                xsl = xt[:, oc * TLOC + tt * 512:
                                         oc * TLOC + tt * 512 + 512]
                                if dbg and sub_idx == 0 and oc == 0 and tt == 0:
                                    v.tensor_copy(tmpr[:, 0:512],
                                                  psO[bp][:, 0:512])
                                if dbg and sub_idx == 0 and oc == 15 and tt == 1:
                                    v.tensor_copy(tmpr[:, 0:512],
                                                  psO[bp][:, 0:512])
                                v.scalar_tensor_tensor(
                                    xsl, psO[bp][:, 0:512], 1.0, xsl,
                                    AL.mult, AL.add).then_inc(s_v.h, 1)
                            res_last = s_v.inc()
                            opp[bp] = res_last
                            if dbg and sub_idx == 0 and oc == 0 and tt == 0:
                                @blk.gpsimd
                                def _(g):
                                    g.wait_ge(s_v.h, res_last)
                                    g.dma_start(out=t_o_d,
                                                in_=tmpr[:, 0:512]).then_inc(
                                                    s_gp.h, 16)
                                    g.dma_start(out=t_wo_d,
                                                in_=wslb[0][:]).then_inc(
                                                    s_gp.h, 16)
                                    g.dma_start(out=t_xm_d,
                                                in_=xt[:, 0:512]).then_inc(
                                                    s_gp.h, 16)
                                s_gp.inc(48)
                            if dbg and sub_idx == 0 and oc == 15 and tt == 1:
                                @blk.gpsimd
                                def _(g):
                                    g.wait_ge(s_v.h, res_last)
                                    g.dma_start(out=t_o2_d,
                                                in_=tmpr[:, 0:512]).then_inc(
                                                    s_gp.h, 16)
                                s_gp.inc(16)
                        oslab_free[par] = pe_ts[-1]
                return res_last

        def emit_ffn(sub_idx, pre_v):
            sfx = f"_s{sub_idx}"
            with ExitStack() as fctx:
                fsb = lambda n, s, d: fctx.enter_context(
                    nc.sbuf_tensor(n + sfx, s, d))
                fps = lambda n, s: fctx.enter_context(
                    nc.psum_tensor(n + sfx, s, F32))
                y_own = fsb("y_own", [128, DCH * TLOC], BF16)
                hT = fsb("hT", [128, HCH * 512], BF16)
                w1sl = [fsb("w1sl0", [128, DCH * 128], BF16),
                        fsb("w1sl1", [128, DCH * 128], BF16)]
                w2sl = [fsb("w2sl0", [128, DCH * 128], BF16),
                        fsb("w2sl1", [128, DCH * 128], BF16)]
                w3sl = [fsb("w3sl0", [128, HCH * 128], BF16),
                        fsb("w3sl1", [128, HCH * 128], BF16)]
                sg = fsb("sg", [128, 512], F32)

                with ExitStack() as lctx:
                    lps = lambda n, s: lctx.enter_context(
                        nc.psum_tensor(n + sfx, s, F32))
                    psS0 = lps("psS0", [1, 512])
                    psS1 = lps("psS1", [1, 512])
                    psB0 = lps("psB0", [128, 512])
                    psB1 = lps("psB1", [128, 512])
                    apply_last, _, _ = emit_ln(hT, psS0, psS1, psB0, psB1, pre_v,
                                               to_agi=False, own_dst=y_own)

                psX1 = [fps("psX10", [128, 512]), fps("psX11", [128, 512])]
                psX2 = [fps("psX20", [128, 512]), fps("psX21", [128, 512])]
                psF = [fps("psF0", [128, 512]), fps("psF1", [128, 512])]

                if dbg and sub_idx == 1:
                    @blk.gpsimd
                    def _(g):
                        if wag_tgts:
                            g.wait_ge(s_cc.h, wag_tgts[-1])
                        g.dma_start(out=t_thr_d, in_=thrv[:, 0:8]).then_inc(
                            s_gp.h, 16)
                        g.dma_start(out=t_wq_d,
                                    in_=wqo[0].ap()[0:128, :]).then_inc(
                                        s_gp.h, 16)
                    s_gp.inc(32)

                w1r = wqo[0].ap().rearrange("(c p) f -> p c f", p=128)
                w2r = wqo[1].ap().rearrange("(c p) f -> p c f", p=128)
                w3r = wqo[2].ap().rearrange("(c p) f -> p c f", p=128)

                res_last = 0
                x12_free = [0, 0]
                sg_free = [0, 0]
                h_free = [0, 0]
                f_free = [0, 0]
                frv_free = [0, 0]
                f_last_prev_tt = 0
                for tt in range(2):
                    for hc in range(HCH):
                        par = hc % 2

                        @blk.sync
                        def _(s):
                            if tt == 0 and hc == 0 and len(wag_tgts) == 3:
                                s.wait_ge(s_cc.h, wag_tgts[1])
                            if x12_free[par]:
                                s.wait_ge(s_pe.h, x12_free[par])
                            s.dma_start(out=w1sl[par][:].rearrange(
                                            "p (c f) -> p c f", f=128),
                                        in_=w1r[:, :, hc * 128:(hc + 1) * 128]
                                        ).then_inc(s_fw[par].h, 16)
                            s.dma_start(out=w2sl[par][:].rearrange(
                                            "p (c f) -> p c f", f=128),
                                        in_=w2r[:, :, hc * 128:(hc + 1) * 128]
                                        ).then_inc(s_fw[par].h, 16)
                        wd = s_fw[par].inc(32)

                        @blk.gpsimd
                        def _(g):
                            g.wait_ge(s_fw[par].h, wd)
                            g.tensor_copy(echo[:, 0:16],
                                          w1sl[par][:, DCH * 128 - 16:DCH * 128])
                            g.tensor_copy(echo[:, 0:16],
                                          w2sl[par][:, DCH * 128 - 16:DCH * 128]
                                          ).then_inc(s_qc.h, 1)
                        wde = s_qc.inc()

                        @blk.tensor
                        def _(t):
                            t.wait_ge(s_qc.h, wde)
                            if tt == 0 and hc == 0:
                                t.wait_ge(s_v.h, apply_last)
                            if sg_free[par]:
                                t.wait_ge(s_sc.h, sg_free[par])
                            last = None
                            for c in range(DCH):
                                last = t.matmul(
                                    psX1[par][:, 0:512],
                                    w1sl[par][:, c * 128:(c + 1) * 128],
                                    y_own[:, c * TLOC + tt * 512:
                                          c * TLOC + tt * 512 + 512],
                                    start=(c == 0), stop=(c == DCH - 1))
                            last.then_inc(s_pe.h, 1)
                        x1_t = s_pe.inc()

                        @blk.scalar
                        def _(s):
                            s.wait_ge(s_pe.h, x1_t)
                            for hf in h_free:
                                if hf:
                                    s.wait_ge(s_v.h, hf)
                            s.activation(sg[:, 0:512],
                                         psX1[par][:, 0:512],
                                         AF.Sigmoid).then_inc(s_sc.h, 1)
                        sg_t = s_sc.inc()
                        sg_free[par] = sg_t

                        @blk.tensor
                        def _(t):
                            if h_free[par]:
                                t.wait_ge(s_v.h, h_free[par])
                            last = None
                            for c in range(DCH):
                                last = t.matmul(
                                    psX2[par][:, 0:512],
                                    w2sl[par][:, c * 128:(c + 1) * 128],
                                    y_own[:, c * TLOC + tt * 512:
                                          c * TLOC + tt * 512 + 512],
                                    start=(c == 0), stop=(c == DCH - 1))
                            last.then_inc(s_pe.h, 1)
                        x2_t = s_pe.inc()
                        x12_free[par] = x2_t

                        @blk.vector
                        def _(v):
                            v.wait_ge(s_pe.h, x2_t)
                            v.wait_ge(s_sc.h, sg_t)
                            if tt == 1 and hc == 0:
                                v.wait_ge(s_pe.h, f_last_prev_tt)
                            if dbg and sub_idx == 1 and tt == 0 and hc == 0:
                                v.tensor_copy(tmpr[:, 0:512], psX1[par][:, 0:512])
                            v.tensor_tensor(hT[:, hc * 512:(hc + 1) * 512],
                                            sg[:, 0:512],
                                            psX2[par][:, 0:512],
                                            AL.mult).then_inc(s_v.h, 1)
                        h_free[par] = s_v.inc()
                        if dbg and sub_idx == 1 and tt == 0 and hc == 0:
                            @blk.gpsimd
                            def _(g):
                                g.wait_ge(s_v.h, h_free[par])
                                g.dma_start(out=t_x1_d,
                                            in_=tmpr[:, 0:512]).then_inc(
                                                s_gp.h, 16)
                                g.dma_start(out=t_h_d,
                                            in_=hT[:, 0:512]).then_inc(
                                                s_gp.h, 16)
                            s_gp.inc(32)

                    h_all = h_free[(HCH - 1) % 2]
                    for oc in range(DCH):
                        par = oc % 2

                        @blk.sync
                        def _(s):
                            if tt == 0 and oc == 0 and len(wag_tgts) == 3:
                                s.wait_ge(s_cc.h, wag_tgts[2])
                            if f_free[par]:
                                s.wait_ge(s_pe.h, f_free[par])
                            for q in range(4):
                                s.dma_start(
                                    out=w3sl[par][:].rearrange(
                                        "p (c f) -> p c f",
                                        f=128)[:, q * 11:(q + 1) * 11, :],
                                    in_=w3r[:, q * 11:(q + 1) * 11,
                                            oc * 128:(oc + 1) * 128]
                                    ).then_inc(s_f3[par].h, 16)
                        w3d = s_f3[par].inc(64)

                        @blk.gpsimd
                        def _(g):
                            g.wait_ge(s_f3[par].h, w3d)
                            g.tensor_copy(echo[:, 0:16],
                                          w3sl[par][:, HCH * 128 - 16:HCH * 128]
                                          ).then_inc(s_qc.h, 1)
                        w3e = s_qc.inc()

                        @blk.tensor
                        def _(t):
                            t.wait_ge(s_qc.h, w3e)
                            if oc == 0:
                                t.wait_ge(s_v.h, h_all)
                            if frv_free[par]:
                                t.wait_ge(s_v.h, frv_free[par])
                            last = None
                            for hc in range(HCH):
                                last = t.matmul(
                                    psF[par][:, 0:512],
                                    w3sl[par][:, hc * 128:(hc + 1) * 128],
                                    hT[:, hc * 512:(hc + 1) * 512],
                                    start=(hc == 0), stop=(hc == HCH - 1))
                            last.then_inc(s_pe.h, 1)
                        f_t = s_pe.inc()
                        f_free[par] = f_t

                        @blk.vector
                        def _(v):
                            v.wait_ge(s_pe.h, f_t)
                            xsl = xt[:, oc * TLOC + tt * 512:
                                     oc * TLOC + tt * 512 + 512]
                            v.scalar_tensor_tensor(
                                xsl, psF[par][:, 0:512], 1.0, xsl,
                                AL.mult, AL.add).then_inc(s_v.h, 1)
                        frv_free[par] = s_v.inc()
                        res_last = frv_free[par]
                    f_last_prev_tt = f_free[(DCH - 1) % 2]
                return res_last

        yag1_tgt = [0]
        gout_tgt = [0]
        pre_v = [(s_in.h, xt_tgt), (s_v.h, consts_tgt), (s_in.h, mask_tgt)]
        res = 0
        for sub in range(n_sub):
            if sub == 0:
                res = emit_attn("a", 0, pre_v)
            elif sub == 1:
                res = emit_ffn(1, [])
            elif sub == 2:
                res = emit_attn("b", 2, [])
            else:
                res = emit_ffn(3, [])
            if dbg:
                @blk.gpsimd
                def _(g):
                    g.wait_ge(s_v.h, res)
                    for c in range(DCH):
                        g.dma_start(out=dbg_d[sub][c * 128:(c + 1) * 128, :],
                                    in_=xt[:, c * TLOC:(c + 1) * TLOC]
                                    ).then_inc(s_gp.h, 16)
                s_gp.inc(16 * DCH)

                @blk.gpsimd
                def _(g):
                    g.wait_ge(s_gp.h, s_gp.v)
            barrier()
            pre_v = []

        @blk.gpsimd
        def _(g):
            g.wait_ge(s_v.h, res)
            for c in range(DCH):
                g.dma_start(out=out_d[c * 128:(c + 1) * 128, :],
                            in_=xt[:, c * TLOC:(c + 1) * TLOC]).then_inc(s_gp.h, 16)
        final_tgt = s_gp.inc(16 * DCH)

        @blk.gpsimd
        def _(g):
            g.wait_ge(s_gp.h, final_tgt)
    return nc


def make_inputs(L0_attn_a_x, L0_attn_a_w_project, L0_attn_a_w_out, L0_attn_b_Wq,
                L0_attn_b_Wk, L0_attn_b_Wv, L0_attn_b_Wo, L0_ffn_w1, L0_ffn_w2,
                L0_ffn_w3):
    x = np.asarray(L0_attn_a_x, dtype=np.float32)
    wpT = np.ascontiguousarray(np.asarray(L0_attn_a_w_project, np.float32).T)
    woaT = np.ascontiguousarray(np.asarray(L0_attn_a_w_out, np.float32).T)
    wqT = np.ascontiguousarray(np.asarray(L0_attn_b_Wq, np.float32).T)
    wkT = np.ascontiguousarray(np.asarray(L0_attn_b_Wk, np.float32).T)
    wvT = np.ascontiguousarray(np.asarray(L0_attn_b_Wv, np.float32).T)
    wobT = np.ascontiguousarray(np.asarray(L0_attn_b_Wo, np.float32).T)
    w1T = np.ascontiguousarray(np.asarray(L0_ffn_w1, np.float32).T)
    w2T = np.ascontiguousarray(np.asarray(L0_ffn_w2, np.float32).T)
    w3T = np.ascontiguousarray(np.asarray(L0_ffn_w3, np.float32).T)

    tri = (np.arange(128)[:, None] <= np.arange(128)[None, :])
    maskA = np.concatenate([tri, np.ones((128, 128), bool)], axis=1)
    maskB = np.concatenate([np.zeros((128, 128), bool), tri], axis=1)
    maskab = np.concatenate([maskA, maskB], axis=1).astype(ml_dtypes.bfloat16)

    in_maps = []
    for c in range(8):
        s, h = c // 2, c % 2
        t0, t1 = h * TLOC, (h + 1) * TLOC
        kcols = []
        for g in range(4 * h, 4 * h + 4):
            kg = wkT[:, g * 64:(g + 1) * 64]
            kcols += [kg, kg]
        wkb = np.ascontiguousarray(np.concatenate(kcols, axis=1))
        wvb = np.ascontiguousarray(wvT[:, h * 256:(h + 1) * 256])
        in_maps.append({
            "xt": np.ascontiguousarray(x[s].T[:, t0:t1]),
            "tok0": np.array([[t0]], dtype=np.uint32),
            "wpq": np.ascontiguousarray(wpT[:, t0:t1]),
            "wpk": np.ascontiguousarray(wpT[:, 2048 + t0:2048 + t1]),
            "wpv": np.ascontiguousarray(wpT[:, 4096 + t0:4096 + t1]),
            "woa": woaT,
            "wqb": np.ascontiguousarray(wqT[:, t0:t1]),
            "wkb": wkb,
            "wvb": wvb,
            "wob": wobT,
            "w1s": np.ascontiguousarray(w1T[256 * c:256 * (c + 1), :]),
            "w2s": np.ascontiguousarray(w2T[256 * c:256 * (c + 1), :]),
            "w3s": np.ascontiguousarray(w3T[704 * c:704 * (c + 1), :]),
            "maskab": maskab,
        })
    return in_maps


_NC_CACHE = {}


def kernel(**inputs):
    key = (4, False)
    if key not in _NC_CACHE:
        _NC_CACHE[key] = build(n_sub=4, dbg=False)
    nc = _NC_CACHE[key]
    in_maps = make_inputs(**inputs)
    res = run_bass_kernel_spmd(nc, in_maps, core_ids=list(range(8)))
    out = np.empty((4, 2048, 2048), dtype=np.float32)
    for c in range(8):
        s, h = c // 2, c % 2
        out[s, h * TLOC:(h + 1) * TLOC, :] = res.results[c]["out_xt"].T
    return out


# version 08
# speedup vs baseline: 1.0021x; 1.0021x over previous
import math
import dataclasses
import numpy as np
import ml_dtypes
from contextlib import ExitStack

import concourse.bass as bass
import concourse.mybir as mybir
from concourse.expressions import RuntimeValue
from concourse.bass_utils import run_bass_kernel_spmd

F32 = mybir.dt.float32
BF16 = mybir.dt.bfloat16
U32 = mybir.dt.uint32
AF = mybir.ActivationFunctionType
AL = mybir.AluOpType
AX = mybir.AxisListType

D = 2048
DCH = 16
TLOC = 1024
TGLB = 2048
NHA = 8
NGR = 4
NST = 8
DFF = 5632
HCH = 44
CEXP = 20.0
SCA = 1.0 / math.sqrt(128.0)
SCB = 1.0 / math.sqrt(64.0)
WCNT = float(DFF * D)
VW = 136

ENGS = ("sync", "tensor", "vector", "scalar", "gpsimd")


class Ctr:
    __slots__ = ("h", "v")

    def __init__(self, h):
        self.h = h
        self.v = 0

    def inc(self, n=1):
        self.v += n
        return self.v


def _bcast_ap(a, n=128):
    return dataclasses.replace(a, ap=[[0, n]] + list(a.ap[1:]))


def build(n_sub=4, dbg=False, no_quant=False, dbg_units=None):
    nc = bass.Bass("TRN2", target_bir_lowering=False, debug=False)

    ap = lambda t: t.ap()
    xt_d = ap(nc.dram_tensor("xt", [D, TLOC], F32, kind="ExternalInput"))
    tok0_d = ap(nc.dram_tensor("tok0", [1, 1], U32, kind="ExternalInput"))
    wpq_d = ap(nc.dram_tensor("wpq", [D, 1024], F32, kind="ExternalInput"))
    wpk_d = ap(nc.dram_tensor("wpk", [D, 1024], F32, kind="ExternalInput"))
    wpv_d = ap(nc.dram_tensor("wpv", [D, 1024], F32, kind="ExternalInput"))
    woa_d = ap(nc.dram_tensor("woa", [D, D], F32, kind="ExternalInput"))
    wqb_d = ap(nc.dram_tensor("wqb", [D, 1024], F32, kind="ExternalInput"))
    wkb_d = ap(nc.dram_tensor("wkb", [D, 512], F32, kind="ExternalInput"))
    wvb_d = ap(nc.dram_tensor("wvb", [D, 256], F32, kind="ExternalInput"))
    wob_d = ap(nc.dram_tensor("wob", [D, D], F32, kind="ExternalInput"))
    w_sl_d = [ap(nc.dram_tensor("w1s", [256, DFF], F32, kind="ExternalInput")),
              ap(nc.dram_tensor("w2s", [256, DFF], F32, kind="ExternalInput")),
              ap(nc.dram_tensor("w3s", [704, D], F32, kind="ExternalInput"))]
    mask_d = ap(nc.dram_tensor("maskab", [128, 512], BF16, kind="ExternalInput"))
    out_d = ap(nc.dram_tensor("out_xt", [D, TLOC], F32, kind="ExternalOutput"))
    dbg_d = [ap(nc.dram_tensor(f"dbg{i}", [D, TLOC], F32, kind="ExternalOutput"))
             for i in range(4)] if dbg else []
    if dbg:
        t_yg_d = ap(nc.dram_tensor("t_yg", [128, DCH * TGLB], BF16,
                                   kind="ExternalOutput"))
        t_qh_d = ap(nc.dram_tensor("t_qh", [128, TGLB], BF16,
                                   kind="ExternalOutput"))
        t_kh_d = ap(nc.dram_tensor("t_kh", [128, TGLB], BF16,
                                   kind="ExternalOutput"))
        t_vh_d = ap(nc.dram_tensor("t_vh", [128, DCH * VW], BF16,
                                   kind="ExternalOutput"))
        t_cx_d = ap(nc.dram_tensor("t_cx", [128, TGLB], BF16,
                                   kind="ExternalOutput"))
        t_p_d = ap(nc.dram_tensor("t_p", [128, 4 * 512], BF16,
                                  kind="ExternalOutput"))
        t_ci_d = ap(nc.dram_tensor("t_ci", [1024, TGLB], BF16,
                                   kind="ExternalOutput"))
        t_o_d = ap(nc.dram_tensor("t_o", [128, 512], F32,
                                  kind="ExternalOutput"))
        t_o2_d = ap(nc.dram_tensor("t_o2", [128, 512], F32,
                                   kind="ExternalOutput"))
        t_xm_d = ap(nc.dram_tensor("t_xm", [128, 512], F32,
                                   kind="ExternalOutput"))
        t_wo_d = ap(nc.dram_tensor("t_wo", [128, DCH * 448], BF16,
                                   kind="ExternalOutput"))
        t_cr_d = ap(nc.dram_tensor("t_cr", [128, DCH * 1024], BF16,
                                   kind="ExternalOutput"))
        t_kh2_d = ap(nc.dram_tensor("t_kh2", [128, TGLB], BF16,
                                    kind="ExternalOutput"))
        t_thr_d = ap(nc.dram_tensor("t_thr", [128, 8], F32,
                                    kind="ExternalOutput"))
        t_ws_d = ap(nc.dram_tensor("t_ws", [128, 4], F32,
                                   kind="ExternalOutput"))
        t_wq_d = ap(nc.dram_tensor("t_wq", [128, DFF], BF16,
                                   kind="ExternalOutput"))
        t_h_d = ap(nc.dram_tensor("t_h", [128, 512], BF16,
                                  kind="ExternalOutput"))
        t_x1_d = ap(nc.dram_tensor("t_x1", [128, 512], F32,
                                   kind="ExternalOutput"))

    y_agi = nc.dram_tensor("y_agi", [D, TLOC], BF16)
    y_ago = nc.dram_tensor("y_ago", [2 * D, TLOC], BF16)
    c_agi = nc.dram_tensor("c_agi", [1024, TGLB], BF16)
    c_ago = nc.dram_tensor("c_ago", [D, TGLB], BF16)
    g_ari = nc.dram_tensor("g_ari", [1, 8], F32)
    g_scr = nc.dram_tensor("g_scr", [128, 4], F32)
    g_aro = nc.dram_tensor("g_aro", [8, 8], F32)
    wqi = [nc.dram_tensor("w1qi", [256, DFF], BF16),
           nc.dram_tensor("w2qi", [256, DFF], BF16),
           nc.dram_tensor("w3qi", [704, D], BF16)]
    wqo = [nc.dram_tensor("w1qo", [D, DFF], BF16),
           nc.dram_tensor("w2qo", [D, DFF], BF16),
           nc.dram_tensor("w3qo", [DFF, D], BF16)]

    RG_PAIR = [[0, 1], [2, 3], [4, 5], [6, 7]]
    RG_WORLD = [list(range(8))]

    st_ctx = ExitStack()
    with st_ctx as ctx:
        sbt = lambda n, s, d: ctx.enter_context(nc.sbuf_tensor(n, s, d))
        sem = lambda n: Ctr(ctx.enter_context(nc.semaphore(n)))

        xt = sbt("xt_sb", [128, DCH * TLOC], F32)
        mask_sb = sbt("mask_sb", [128, 512], BF16)
        ones_bf = sbt("ones_bf", [128, 1], BF16)
        onesr_bf = sbt("onesr_bf", [1, 128], BF16)
        onesr_f = sbt("onesr_f", [1, 128], F32)
        bm20 = sbt("bm20", [128, 1], F32)
        epsln = sbt("epsln", [1, 1], F32)
        thrv = sbt("thrv", [128, 8], F32)
        gsb = sbt("gsb", [1, 8], F32)
        gred = sbt("gred", [1, 512], F32)
        echo = sbt("echo", [128, 16], BF16)
        wsum = sbt("wsum", [128, 4], F32)
        wred = sbt("wred", [128, 1], F32)
        tok0_sb = sbt("tok0_sb", [1, 1], U32)
        m_t = sbt("m_t", [1, 512], F32)
        q_t = sbt("q_t", [1, 512], F32)
        rs_t = sbt("rs_t", [1, 512], F32)
        mr_t = sbt("mr_t", [1, 512], F32)
        sqr = sbt("sqr", [128, 2 * 512], BF16)
        tmpr = sbt("tmpr", [128, 512], F32)
        lnr = sbt("lnr", [128, 4 * 512], BF16)
        qst = sbt("qst", [128, 2 * 256], F32)
        qmp = sbt("qmp", [128, 256], BF16)
        qmn = sbt("qmn", [128, 256], BF16)
        qob = sbt("qob", [128, 2 * 256], BF16)
        rr = sbt("rr", [1, 512], BF16)
        ctxf = sbt("ctxf", [128, 256], F32)

        s_in = sem("s_in")
        s_gp = sem("s_gp")
        s_cc = sem("s_cc")
        s_pe = sem("s_pe")
        s_v = sem("s_v")
        s_sc = sem("s_sc")
        s_bar = sem("s_bar")
        s_qc = sem("s_qc")
        s_sd = sem("s_sd")
        s_qt = [sem("s_qt0"), sem("s_qt1")]
        s_qb = [sem("s_qb0"), sem("s_qb1")]
        s_fw = [sem("s_fw0"), sem("s_fw1")]
        s_f3 = [sem("s_f30"), sem("s_f31")]

        blk = ctx.enter_context(nc.Block())

        @blk.vector
        def _(v):
            v.memset(ones_bf[:], 1.0)
            v.memset(onesr_bf[:], 1.0)
            v.memset(onesr_f[:], 1.0)
            v.memset(bm20[:], -CEXP)
            v.memset(epsln[:], 1e-5)
            v.memset(wsum[:], 0.0)
            v.memset(gsb[:], 0.0).then_inc(s_v.h, 1)
        consts_tgt = s_v.inc()

        @blk.sync
        def _(s):
            s.dma_start(out=tok0_sb[:], in_=tok0_d).then_inc(s_in.h, 16)
            s.dma_start(out=mask_sb[:], in_=mask_d).then_inc(s_in.h, 16)
            for c in range(DCH):
                s.dma_start(out=xt[:, c * TLOC:(c + 1) * TLOC],
                            in_=xt_d[c * 128:(c + 1) * 128, :]).then_inc(s_in.h, 16)
        tok_tgt = s_in.inc(16)
        mask_tgt = s_in.inc(16)
        xt_tgt = s_in.inc(16 * DCH)


        def barrier():
            for e in ENGS:
                @getattr(blk, e)
                def _(eng):
                    eng.nop().then_inc(s_bar.h, 1)
            tgt = s_bar.inc(5)
            for e in ENGS:
                @getattr(blk, e)
                def _(eng):
                    eng.wait_ge(s_bar.h, tgt)
            return tgt

        yago_read = [0]
        cago_read = [0]
        wag_tgts = []
        quant_emitted = [False]
        qc_used = [0, 0]
        allq_tgt = [0]
        quant_thr_tgt = [0]
        abs_state = [0, 0]

        def emit_ln(xb_scr, psS0, psS1, psB0, psB1, pre_v, to_agi, own_dst,
                    agi_pre_cc=0):
            @blk.vector
            def _(v):
                for (h, t) in pre_v:
                    v.wait_ge(h, t)
                v.tensor_copy(xb_scr[:, 0:DCH * TLOC], xt[:]).then_inc(s_v.h, 1)
            cast_tgt = s_v.inc()

            dma_tgts = [0] * (2 * DCH)
            apply_last = 0
            stats_read = 0
            sq_pe = {}
            for tt in range(2):
                for c in range(DCH):
                    i = tt * DCH + c
                    slot = i % 2
                    xb_sl = xb_scr[:, c * TLOC + tt * 512: c * TLOC + tt * 512 + 512]

                    @blk.scalar
                    def _(s):
                        if i == 0:
                            s.wait_ge(s_v.h, cast_tgt)
                        if i >= 2:
                            s.wait_ge(s_pe.h, sq_pe[i - 2])
                        s.activation(sqr[:, slot * 512:slot * 512 + 512], xb_sl,
                                     AF.Square).then_inc(s_sc.h, 1)
                    sc_t = s_sc.inc()

                    @blk.tensor
                    def _(t):
                        if i == 0:
                            t.wait_ge(s_v.h, cast_tgt)
                        if c == 0 and stats_read:
                            t.wait_ge(s_v.h, stats_read)
                        t.matmul(psS0[0:1, 0:512], ones_bf[:], xb_sl,
                                 start=(c == 0), stop=(c == DCH - 1))
                        t.wait_ge(s_sc.h, sc_t)
                        mm = t.matmul(psS1[0:1, 0:512], ones_bf[:],
                                      sqr[:, slot * 512:slot * 512 + 512],
                                      start=(c == 0), stop=(c == DCH - 1))
                        mm.then_inc(s_pe.h, 1)
                    sq_pe[i] = s_pe.inc()
                sums_done = sq_pe[tt * DCH + DCH - 1]

                @blk.vector
                def _(v):
                    v.wait_ge(s_pe.h, sums_done)
                    v.tensor_scalar(m_t[0:1, :], psS0[0:1, 0:512], 1.0 / D, None,
                                    AL.mult)
                    v.tensor_scalar(q_t[0:1, :], psS1[0:1, 0:512], 1.0 / D, None,
                                    AL.mult).then_inc(s_v.h, 1)
                    v.wait_ge(s_v.h, s_v.inc())
                    v.tensor_tensor(mr_t[0:1, :], m_t[0:1, :], m_t[0:1, :],
                                    AL.mult).then_inc(s_v.h, 1)
                    v.wait_ge(s_v.h, s_v.inc())
                    v.tensor_tensor(q_t[0:1, :], q_t[0:1, :], mr_t[0:1, :],
                                    AL.subtract).then_inc(s_v.h, 1)
                stats_read = s_v.inc()

                @blk.scalar
                def _(s):
                    s.wait_ge(s_v.h, stats_read)
                    s.activation(mr_t[0:1, :], q_t[0:1, :], AF.Ln,
                                 bias=epsln[0:1, 0:1])
                    s.activation(rs_t[0:1, :], mr_t[0:1, :], AF.Exp,
                                 scale=-0.5).then_inc(s_sc.h, 1)
                rstd_tgt = s_sc.inc()

                @blk.vector
                def _(v):
                    v.wait_ge(s_sc.h, rstd_tgt)
                    v.scalar_tensor_tensor(mr_t[0:1, :], m_t[0:1, :], -1.0,
                                           rs_t[0:1, :], AL.mult,
                                           AL.mult).then_inc(s_v.h, 1)
                mr_tgt = s_v.inc()

                @blk.tensor
                def _(t):
                    t.wait_ge(s_v.h, mr_tgt)
                    if apply_last:
                        t.wait_ge(s_v.h, apply_last)
                    t.matmul(psB0[:, 0:512], onesr_f[:], rs_t[0:1, :],
                             start=True, stop=True)
                    t.matmul(psB1[:, 0:512], onesr_f[:], mr_t[0:1, :],
                             start=True, stop=True).then_inc(s_pe.h, 1)
                bc_tgt = s_pe.inc()

                for c in range(DCH):
                    i = tt * DCH + c
                    slot = i % 4
                    xt_sl = xt[:, c * TLOC + tt * 512: c * TLOC + tt * 512 + 512]

                    @blk.vector
                    def _(v):
                        if c == 0:
                            v.wait_ge(s_pe.h, bc_tgt)
                        if to_agi and i >= 4:
                            v.wait_ge(s_gp.h, dma_tgts[i - 4])
                        v.tensor_tensor(tmpr[:, 0:512],
                                        xt_sl, psB0[:, 0:512], AL.mult)
                        if to_agi:
                            dst = lnr[:, slot * 512:slot * 512 + 512]
                        else:
                            dst = own_dst[:, c * TLOC + tt * 512:
                                          c * TLOC + tt * 512 + 512]
                        v.tensor_tensor(dst, tmpr[:, 0:512],
                                        psB1[:, 0:512], AL.add).then_inc(s_v.h, 1)
                    a_t = s_v.inc()
                    apply_last = a_t

                    if to_agi:
                        @blk.gpsimd
                        def _(g):
                            g.wait_ge(s_v.h, a_t)
                            if i == 0 and agi_pre_cc:
                                g.wait_ge(s_cc.h, agi_pre_cc)
                            g.dma_start(out=y_agi.ap()[c * 128:(c + 1) * 128,
                                                       tt * 512:tt * 512 + 512],
                                        in_=lnr[:, slot * 512:slot * 512 + 512]
                                        ).then_inc(s_gp.h, 16)
                        dma_tgts[i] = s_gp.inc(16)
            return apply_last, (dma_tgts[-1] if to_agi else 0), stats_read

        def q_tiles():
            tiles = []
            for m in range(3):
                if m < 2:
                    rows, cols = 256, DFF
                else:
                    rows, cols = 704, D
                r0 = 0
                while r0 < rows:
                    pr = min(128, rows - r0)
                    for c0 in range(0, cols, 256):
                        tiles.append((m, r0, pr, c0))
                    r0 += pr
            return tiles

        def emit_quant_abs(stats_free_tgt):
            tiles = q_tiles()
            vtail = [0, 0]
            for i, (m, r0, pr, c0) in enumerate(tiles):
                par = i % 2

                @blk.sync
                def _(s):
                    if vtail[par]:
                        s.wait_ge(s_v.h, vtail[par])
                    s.dma_start(out=qst[0:pr, par * 256:par * 256 + 256],
                                in_=w_sl_d[m][r0:r0 + pr, c0:c0 + 256]
                                ).then_inc(s_qt[par].h, 16)
                d_t = s_qt[par].inc(16)

                @blk.gpsimd
                def _(g):
                    g.wait_ge(s_qt[par].h, d_t)
                    g.tensor_copy(echo[0:pr, 0:16],
                                  qst[0:pr, par * 256 + 240:par * 256 + 256]
                                  ).then_inc(s_qc.h, 1)
                e_t = s_qc.inc()

                @blk.vector
                def _(v):
                    v.wait_ge(s_qc.h, e_t)
                    v.tensor_reduce(wred[0:pr, 0:1],
                                    qst[0:pr, par * 256:par * 256 + 256],
                                    AX.X, AL.add,
                                    apply_absolute_value=True).then_inc(s_v.h, 1)
                red_t = s_v.inc()

                @blk.vector
                def _(v):
                    v.wait_ge(s_v.h, red_t)
                    v.tensor_tensor(wsum[0:pr, m:m + 1], wsum[0:pr, m:m + 1],
                                    wred[0:pr, 0:1], AL.add).then_inc(s_v.h, 1)
                vtail[par] = s_v.inc()
            abs_done = vtail[(len(tiles) - 1) % 2]

            @blk.gpsimd
            def _(g):
                g.wait_ge(s_v.h, abs_done)
                g.dma_start(out=g_scr.ap(), in_=wsum[:, 0:4]).then_inc(s_gp.h, 16)
                if dbg:
                    g.dma_start(out=t_ws_d, in_=wsum[:, 0:4]).then_inc(s_gp.h, 16)
            g_s1 = s_gp.inc(16)
            if dbg:
                s_gp.inc(16)

            @blk.gpsimd
            def _(g):
                g.wait_ge(s_gp.h, g_s1)
                g.dma_start(out=gred[0:1, :],
                            in_=g_scr.ap().rearrange("r m -> (r m)")[None, :]
                            ).then_inc(s_gp.h, 16)
            g_s2 = s_gp.inc(16)

            @blk.gpsimd
            def _(g):
                g.wait_ge(s_gp.h, g_s2)
                w = 256
                while w >= 4:
                    g.tensor_tensor(gred[0:1, 0:w], gred[0:1, 0:w],
                                    gred[0:1, w:2 * w],
                                    AL.add).then_inc(s_qc.h, 1)
                    g.wait_ge(s_qc.h, s_qc.inc())
                    w //= 2
                g.tensor_copy(gsb[0:1, 0:4], gred[0:1, 0:4]).then_inc(s_qc.h, 1)
            g_v = s_qc.inc()

            @blk.gpsimd
            def _(g):
                g.wait_ge(s_qc.h, g_v)
                g.dma_start(out=g_ari.ap(), in_=gsb[0:1, 0:8]).then_inc(s_gp.h, 16)
            g_d = s_gp.inc(16)

            @blk.gpsimd
            def _(g):
                g.wait_ge(s_gp.h, g_d)
                g.collective_compute("AllGather", AL.bypass, replica_groups=RG_WORLD,
                                     ins=[g_ari.ap().opt()],
                                     outs=[g_aro.ap().opt()]).then_inc(s_cc.h, 1)
            ar_tgt = s_cc.inc()

            @blk.gpsimd
            def _(g):
                g.wait_ge(s_cc.h, ar_tgt)
                g.dma_start(out=gred[0:1, 0:64],
                            in_=g_aro.ap().rearrange("r m -> (r m)")[None, :]
                            ).then_inc(s_gp.h, 16)
            gg_t = s_gp.inc(16)

            @blk.gpsimd
            def _(g):
                g.wait_ge(s_gp.h, gg_t)
                g.tensor_copy(echo[0:1, 0:16], gred[0:1, 48:64]).then_inc(s_qc.h, 1)
                g.wait_ge(s_qc.h, s_qc.inc())
                w = 32
                while w >= 8:
                    g.tensor_tensor(gred[0:1, 0:w], gred[0:1, 0:w],
                                    gred[0:1, w:2 * w],
                                    AL.add).then_inc(s_qc.h, 1)
                    g.wait_ge(s_qc.h, s_qc.inc())
                    w //= 2
                g.tensor_copy(gsb[0:1, 0:8], gred[0:1, 0:8]).then_inc(s_qc.h, 1)
            gf_t = s_qc.inc()

            @blk.gpsimd
            def _(g):
                g.wait_ge(s_qc.h, gf_t)
                g.nop().then_inc(s_gp.h, 16)
            return s_gp.inc(16), vtail[(len(tiles) - 1) % 2]

        q_dtail = [0, 0]

        def emit_quant_thr(gout_tgt):
            @blk.gpsimd
            def _(g):
                g.wait_ge(s_gp.h, gout_tgt)
                g.tensor_scalar(gsb[0:1, 0:4], gsb[0:1, 0:4], 1.0 / WCNT, None,
                                AL.mult)
                g.tensor_scalar(gsb[0:1, 0:4], gsb[0:1, 0:4], 1e-8, None, AL.add)
                g.tensor_scalar(gsb[0:1, 0:4], gsb[0:1, 0:4], 0.5, None, AL.mult)
                g.tensor_scalar(gsb[0:1, 4:8], gsb[0:1, 0:4], -1.0, None,
                                AL.mult).then_inc(s_qc.h, 1)
            thr_c = s_qc.inc()

            @blk.gpsimd
            def _(g):
                g.wait_ge(s_qc.h, thr_c)
                g.dma_start(out=g_ari.ap(), in_=gsb[0:1, 0:8]).then_inc(s_gp.h, 16)
            thr_d = s_gp.inc(16)

            @blk.gpsimd
            def _(g):
                g.wait_ge(s_gp.h, thr_d)
                g.dma_start(out=thrv[:, 0:8],
                            in_=_bcast_ap(g_ari.ap())).then_inc(s_gp.h, 16)
            quant_thr_tgt[0] = s_gp.inc(16)

        def emit_quant_tiles(lo, hi, abs_vtail):
            tiles = q_tiles()
            for i in range(lo, min(hi, len(tiles))):
                m, r0, pr, c0 = tiles[i]
                par = i % 2

                @blk.gpsimd
                def _(g):
                    if i == 0:
                        g.wait_ge(s_v.h, abs_vtail)
                        g.wait_ge(s_gp.h, quant_thr_tgt[0])
                    if i >= 2:
                        g.wait_ge(s_qc.h, qc_used[par])
                    g.dma_start(out=qst[0:pr, par * 256:par * 256 + 256],
                                in_=w_sl_d[m][r0:r0 + pr, c0:c0 + 256]
                                ).then_inc(s_qt[par].h, 16)
                d_t = s_qt[par].inc(16)

                @blk.gpsimd
                def _(g):
                    g.wait_ge(s_qt[par].h, d_t)
                    if q_dtail[par]:
                        g.wait_ge(s_qb[par].h, q_dtail[par])
                    g.tensor_copy(echo[0:pr, 0:16],
                                  qst[0:pr, par * 256 + 240:par * 256 + 256]
                                  ).then_inc(s_qc.h, 1)
                    g.wait_ge(s_qc.h, s_qc.inc())
                    g.tensor_scalar(qmp[0:pr, 0:256],
                                    qst[0:pr, par * 256:par * 256 + 256],
                                    thrv[0:pr, m:m + 1], None, AL.is_ge)
                    g.tensor_scalar(qmn[0:pr, 0:256],
                                    qst[0:pr, par * 256:par * 256 + 256],
                                    thrv[0:pr, 4 + m:5 + m], None,
                                    AL.is_le).then_inc(s_qc.h, 1)
                    g.wait_ge(s_qc.h, s_qc.inc())
                    g.tensor_tensor(qob[0:pr, par * 256:par * 256 + 256],
                                    qmp[0:pr, 0:256], qmn[0:pr, 0:256],
                                    AL.subtract).then_inc(s_qc.h, 1)
                qc_used[par] = s_qc.inc()

                @blk.gpsimd
                def _(g):
                    g.wait_ge(s_qc.h, qc_used[par])
                    g.dma_start(out=wqi[m].ap()[r0:r0 + pr, c0:c0 + 256],
                                in_=qob[0:pr, par * 256:par * 256 + 256]
                                ).then_inc(s_qb[par].h, 16)
                q_dtail[par] = s_qb[par].inc(16)
            if hi >= len(tiles):
                allq_tgt[0] = (s_qb[0].v, s_qb[1].v)

        def emit_wag(m):
            @blk.gpsimd
            def _(g):
                g.wait_ge(s_qb[0].h, allq_tgt[0][0])
                g.wait_ge(s_qb[1].h, allq_tgt[0][1])
                g.collective_compute("AllGather", AL.bypass,
                                     replica_groups=RG_WORLD,
                                     ins=[wqi[m].ap().opt()],
                                     outs=[wqo[m].ap().opt()]
                                     ).then_inc(s_cc.h, 1)
            wag_tgts.append(s_cc.inc())

        def emit_attn(kind, sub_idx, pre_v):
            sfx = f"_s{sub_idx}"
            with ExitStack() as actx:
                asb = lambda n, s, d: actx.enter_context(
                    nc.sbuf_tensor(n + sfx, s, d))
                qh = asb("qh", [128, TGLB], BF16)
                kh = asb("kh", [128, TGLB], BF16)
                vh = asb("vh", [128, DCH * VW], BF16)
                wslb = [asb("wslb0", [128, DCH * 448], BF16),
                        asb("wslb1", [128, DCH * 448], BF16)]
                p_sb = asb("p_sb", [128, 4 * 512], BF16)
                ctxs = asb("ctxs", [128, TGLB], BF16)

                scale = SCA if kind == "a" else SCB
                nunits = NHA if kind == "a" else NGR
                if dbg_units:
                    nunits = min(nunits, dbg_units)
                res_last = 0

                with ExitStack() as yctx:
                    ysb = lambda n, s, d: yctx.enter_context(
                        nc.sbuf_tensor(n + sfx, s, d))
                    yps = lambda n, s: yctx.enter_context(
                        nc.psum_tensor(n + sfx, s, F32))
                    yg = ysb("yg", [128, DCH * TGLB], BF16)

                    with ExitStack() as lctx:
                        lps = lambda n, s: lctx.enter_context(
                            nc.psum_tensor(n + sfx, s, F32))
                        psS0 = lps("psS0", [1, 512])
                        psS1 = lps("psS1", [1, 512])
                        psB0 = lps("psB0", [128, 512])
                        psB1 = lps("psB1", [128, 512])
                        apply_last, ln_dma, stats_free = emit_ln(
                            yg, psS0, psS1, psB0, psB1, pre_v,
                            to_agi=True, own_dst=None,
                            agi_pre_cc=(yag1_tgt[0] if sub_idx == 2 else 0))

                        @blk.gpsimd
                        def _(g):
                            g.wait_ge(s_gp.h, ln_dma)
                            if sub_idx == 2:
                                g.wait_ge(s_in.h, yago_read[0])
                            g.collective_compute(
                                "AllGather", AL.bypass, replica_groups=RG_PAIR,
                                ins=[y_agi.ap().opt()],
                                outs=[y_ago.ap().opt()]).then_inc(s_cc.h, 1)
                        yag = s_cc.inc()
                        if sub_idx == 0:
                            yag1_tgt[0] = yag

                        if sub_idx == 0 and not quant_emitted[0] and not no_quant:
                            abs_state[0], abs_state[1] = emit_quant_abs(
                                stats_free)

                    @blk.scalar
                    def _(s):
                        s.wait_ge(s_cc.h, yag)
                        for hh in range(2):
                            for c in range(DCH):
                                s.dma_start(
                                    out=yg[:, c * TGLB + hh * TLOC:
                                           c * TGLB + hh * TLOC + TLOC],
                                    in_=y_ago.ap()[hh * D + c * 128:
                                                   hh * D + c * 128 + 128, :]
                                    ).then_inc(s_in.h, 16)
                    yg_tgt = s_in.inc(16 * 2 * DCH)
                    yago_read[0] = yg_tgt

                    psP = [yps("psP0", [128, 512]), yps("psP1", [128, 512])]
                    if kind == "a":
                        psS = [yps("psSc0", [128, 512]), yps("psSc1", [128, 512])]
                        psC = [yps("psC0", [128, 512]), yps("psC1", [128, 512])]
                        psD = [yps("psD0", [1, 512]), yps("psD1", [1, 512])]
                    else:
                        psS = [yps("psSc0", [128, 512]), yps("psSc1", [128, 512]),
                               yps("psSc2", [128, 512]), yps("psSc3", [128, 512])]
                        psC = [yps("psC0", [128, 512]), yps("psC1", [128, 512])]

                    if kind == "b":
                        @blk.vector
                        def _(v):
                            vv = vh[:].rearrange("p (tc w) -> p tc w", w=VW)
                            v.memset(vv[:, :, 64:65], 1.0).then_inc(s_v.h, 1)
                        s_v.inc()

                    pp = [0, 0]
                    slab_free = [0, 0]
                    stail = [0, 0]
                    av_tail = 0
                    ctx_dma = 0
                    cago_dmas = []
                    sc_free = {}
                    p_free = {}
                    cb_free = [0, 0]
                    db_free = [0, 0]
                    norm_sc = [0, 0]
                    bc_prev = [0]

                    def emit_proj(par, col, width, dst_sb, n_out, first_waits,
                                  v_first_waits):
                        tail = 0
                        pe_t = 0
                        for tt4 in range(n_out):
                            bp = tt4 % 2

                            @blk.tensor
                            def _(t):
                                if tt4 == 0:
                                    for (h, tg) in first_waits:
                                        t.wait_ge(h, tg)
                                if pp[bp]:
                                    t.wait_ge(s_v.h, pp[bp])
                                last = None
                                for c in range(DCH):
                                    last = t.matmul(
                                        psP[bp][:, 0:512],
                                        wslb[par][:, c * 448 + col:
                                                  c * 448 + col + width],
                                        yg[:, c * TGLB + tt4 * 512:
                                           c * TGLB + tt4 * 512 + 512],
                                        start=(c == 0), stop=(c == DCH - 1))
                                last.then_inc(s_pe.h, 1)
                            pe_t = s_pe.inc()

                            @blk.vector
                            def _(v):
                                v.wait_ge(s_pe.h, pe_t)
                                if tt4 == 0:
                                    for (h, tg) in v_first_waits:
                                        v.wait_ge(h, tg)
                                v.tensor_copy(dst_sb[:, tt4 * 512:tt4 * 512 + 512],
                                              psP[bp][:, 0:512]).then_inc(s_v.h, 1)
                            tail = s_v.inc()
                            pp[bp] = tail
                        return tail, pe_t

                    def emit_vproj(par, col, vwid, first_waits, v_first_waits):
                        tail = 0
                        for tc in range(DCH):
                            bp = tc % 2

                            @blk.tensor
                            def _(t):
                                if tc == 0:
                                    for (h, tg) in first_waits:
                                        t.wait_ge(h, tg)
                                if pp[bp]:
                                    t.wait_ge(s_v.h, pp[bp])
                                last = None
                                for c in range(DCH):
                                    last = t.matmul(
                                        psP[bp][:, 0:vwid],
                                        yg[:, c * TGLB + tc * 128:
                                           c * TGLB + tc * 128 + 128],
                                        wslb[par][:, c * 448 + col:
                                                  c * 448 + col + vwid],
                                        start=(c == 0), stop=(c == DCH - 1))
                                last.then_inc(s_pe.h, 1)
                            pe_t = s_pe.inc()

                            @blk.vector
                            def _(v):
                                v.wait_ge(s_pe.h, pe_t)
                                if tc == 0:
                                    for (h, tg) in v_first_waits:
                                        v.wait_ge(h, tg)
                                v.tensor_copy(vh[:, tc * VW:tc * VW + vwid],
                                              psP[bp][:, 0:vwid]).then_inc(s_v.h, 1)
                            tail = s_v.inc()
                            pp[bp] = tail
                        return tail

                    if dbg and sub_idx == 0:
                        @blk.gpsimd
                        def _(g):
                            g.wait_ge(s_in.h, yg_tgt)
                            g.dma_start(out=t_yg_d, in_=yg[:]).then_inc(s_gp.h, 16)
                        s_gp.inc(16)

                    def emit_slab(uu):
                        upar = uu % 2
                        if kind == "a":
                            usrcs = [(wpq_d, uu * 128, 128),
                                     (wpk_d, uu * 128, 128),
                                     (wpv_d, uu * 128, 128)]
                        else:
                            usrcs = [(wqb_d, uu * 256, 256),
                                     (wkb_d, uu * 128, 128),
                                     (wvb_d, uu * 64, 64)]

                        @blk.gpsimd
                        def _(g):
                            if slab_free[upar]:
                                g.wait_ge(s_pe.h, slab_free[upar])
                            wv = wslb[upar][:].rearrange("p (c q) -> p c q", q=448)
                            off = 0
                            for (dap, col0, w) in usrcs:
                                g.dma_start(
                                    out=wv[:, :, off:off + w],
                                    in_=dap.rearrange("(c p) o -> p c o", p=128)
                                    [:, :, col0:col0 + w]).then_inc(s_gp.h, 16)
                                off += w
                        dma_t = s_gp.inc(16 * len(usrcs))

                        @blk.gpsimd
                        def _(g):
                            g.wait_ge(s_gp.h, dma_t)
                            sw = sum(w for (_, _, w) in usrcs)
                            g.tensor_copy(echo[:, 0:16],
                                          wslb[upar][:, (DCH - 1) * 448 + sw - 16:
                                                     (DCH - 1) * 448 + sw]
                                          ).then_inc(s_qc.h, 1)
                        return s_qc.inc()

                    cast_tgts = {0: emit_slab(0)}
                    n_qt = len(q_tiles())
                    qchunk = (n_qt + nunits - 2) // max(nunits - 1, 1)

                    for u in range(nunits):
                        par = u % 2
                        if kind == "a":
                            kcol, vcol, vwid = 128, 256, 128
                            njp = 1
                        else:
                            kcol, vcol, vwid = 256, 384, 64
                            njp = 2

                        if u + 1 < nunits:
                            cast_tgts[u + 1] = emit_slab(u + 1)
                        if sub_idx == 0 and not no_quant:
                            if u == 0:
                                emit_quant_thr(abs_state[0])
                            if u >= 1:
                                emit_quant_tiles((u - 1) * qchunk, u * qchunk,
                                                 abs_state[1])
                            if u == nunits - 1:
                                emit_quant_tiles((u) * qchunk, n_qt,
                                                 abs_state[1])
                                emit_wag(0)

                        fw = [(s_qc.h, cast_tgts[u]), (s_in.h, yg_tgt)]
                        vw = [(s_pe.h, av_tail)] if av_tail else []
                        kh_tail, _ = emit_proj(par, kcol, 128, kh, 4, fw, vw)
                        if dbg and sub_idx == 0 and u == 0:
                            @blk.gpsimd
                            def _(g):
                                g.wait_ge(s_v.h, kh_tail)
                                g.dma_start(out=t_kh2_d, in_=kh[:]).then_inc(
                                    s_gp.h, 16)
                            s_gp.inc(16)
                        vh_tail = emit_vproj(par, vcol, vwid, [], vw)

                        q_pe_last = 0
                        for jp in range(njp):
                            qcol = 0 if kind == "a" else jp * 128
                            vwq = [(s_pe.h, av_tail)] if av_tail else []
                            qh_tail, q_pe_last = emit_proj(par, qcol, 128, qh, 4,
                                                           [], vwq)
                            last_avs = {}
                            qkv_wait_done = False

                            for st in range(NST):
                                npairs = st + 1
                                stp = st % 2
                                for pb in range(npairs):
                                    b0, b1 = 2 * pb, 2 * pb + 1
                                    spar = pb % 2
                                    diag = (pb == npairs - 1)

                                    @blk.tensor
                                    def _(t):
                                        if not qkv_wait_done:
                                            t.wait_ge(s_v.h, qh_tail)
                                            t.wait_ge(s_v.h, kh_tail)
                                            t.wait_ge(s_v.h, vh_tail)
                                        for sj in range(2 if kind == "b" else 1):
                                            sidx = (2 * spar + sj) if kind == "b" \
                                                else spar
                                            if sc_free.get(sidx):
                                                t.wait_ge(s_sc.h, sc_free[sidx])
                                            if kind == "a":
                                                t.matmul(psS[sidx][:, 0:256],
                                                         kh[:, b0 * 128:b0 * 128 + 128],
                                                         qh[:, st * 256:st * 256 + 256],
                                                         start=True, stop=False)
                                                mm = t.matmul(
                                                    psS[sidx][:, 256:512],
                                                    kh[:, b1 * 128:b1 * 128 + 128],
                                                    qh[:, st * 256:st * 256 + 256],
                                                    start=False, stop=True)
                                            else:
                                                p0 = sj * 64
                                                tp = (p0, 0)
                                                t.matmul(psS[sidx][:, 0:256],
                                                         kh[p0:p0 + 64,
                                                            b0 * 128:b0 * 128 + 128],
                                                         qh[p0:p0 + 64,
                                                            st * 256:st * 256 + 256],
                                                         start=True, stop=False,
                                                         tile_position=tp)
                                                mm = t.matmul(
                                                    psS[sidx][:, 256:512],
                                                    kh[p0:p0 + 64,
                                                       b1 * 128:b1 * 128 + 128],
                                                    qh[p0:p0 + 64,
                                                       st * 256:st * 256 + 256],
                                                    start=False, stop=True,
                                                    tile_position=tp)
                                            mm.then_inc(s_pe.h, 1)
                                    qkv_wait_done = True
                                    qk_t = []
                                    for sj in range(2 if kind == "b" else 1):
                                        qk_t.append(s_pe.inc())

                                    av_waits = []
                                    for sj in range(2 if kind == "b" else 1):
                                        sidx = (2 * spar + sj) if kind == "b" \
                                            else spar
                                        slot = sidx if kind == "b" else spar

                                        @blk.scalar
                                        def _(s):
                                            s.wait_ge(s_pe.h, qk_t[sj])
                                            if p_free.get(slot):
                                                s.wait_ge(s_pe.h, p_free[slot])
                                            s.activation(
                                                p_sb[:, slot * 512:slot * 512 + 512],
                                                psS[sidx][:, 0:512], AF.Exp,
                                                bias=bm20[:, 0:1],
                                                scale=scale).then_inc(s_sc.h, 1)
                                        e_t = s_sc.inc()
                                        sc_free[sidx] = e_t
                                        if diag:
                                            @blk.vector
                                            def _(v):
                                                v.wait_ge(s_sc.h, e_t)
                                                v.tensor_tensor(
                                                    p_sb[:, slot * 512:
                                                         slot * 512 + 512],
                                                    p_sb[:, slot * 512:
                                                         slot * 512 + 512],
                                                    mask_sb[:],
                                                    AL.mult).then_inc(s_v.h, 1)
                                            av_waits.append((s_v.h, s_v.inc()))
                                        else:
                                            av_waits.append((s_sc.h, e_t))

                                    @blk.tensor
                                    def _(t):
                                        for (h, tg) in av_waits:
                                            t.wait_ge(h, tg)
                                        if pb == 0:
                                            if cb_free[stp]:
                                                t.wait_ge(s_v.h, cb_free[stp])
                                            if kind == "a" and db_free[stp]:
                                                t.wait_ge(s_v.h, db_free[stp])
                                        last = None
                                        for sj in range(2 if kind == "b" else 1):
                                            slot = (2 * spar + sj) if kind == "b" \
                                                else spar
                                            co = sj * 256
                                            vb = 65 if kind == "b" else 128
                                            first = (pb == 0 and sj == 0)
                                            for (bb, po) in ((b0, 0), (b1, 256)):
                                                last = t.matmul(
                                                    psC[stp][0:65, co:co + 256]
                                                    if kind == "b" else
                                                    psC[stp][:, 0:256],
                                                    vh[:, bb * VW:bb * VW + vb],
                                                    p_sb[:, slot * 512 + po:
                                                         slot * 512 + po + 256],
                                                    start=(first and bb == b0),
                                                    stop=(diag and bb == b1))
                                        if kind == "a":
                                            for (bb, po) in ((b0, 0), (b1, 256)):
                                                last = t.matmul(
                                                    psD[stp][0:1, 0:256],
                                                    ones_bf[:],
                                                    p_sb[:, spar * 512 + po:
                                                         spar * 512 + po + 256],
                                                    start=(pb == 0 and bb == b0),
                                                    stop=(diag and bb == b1))
                                        last.then_inc(s_pe.h, 1)
                                    av_t = s_pe.inc()
                                    for sj in range(2 if kind == "b" else 1):
                                        slot = (2 * spar + sj) if kind == "b" \
                                            else spar
                                        p_free[slot] = av_t
                                    av_tail = av_t
                                last_avs[st] = av_tail

                                @blk.vector
                                def _(v):
                                    v.wait_ge(s_pe.h, last_avs[st])
                                    if bc_prev[0]:
                                        v.wait_ge(s_pe.h, bc_prev[0])
                                    with nc.allow_low_precision(
                                            reason="softmax 1/sum in bf16 for "
                                                   "the broadcast matmul"):
                                        if kind == "a":
                                            v.reciprocal(rr[0:1, 0:256],
                                                         psD[stp][0:1, 0:256]
                                                         ).then_inc(s_v.h, 1)
                                        else:
                                            v.reciprocal(rr[0:1, 0:256],
                                                         psC[stp][64:65, 0:256])
                                            v.reciprocal(rr[0:1, 256:512],
                                                         psC[stp][64:65, 256:512]
                                                         ).then_inc(s_v.h, 1)
                                rr_t = s_v.inc()
                                if kind == "a":
                                    db_free[stp] = rr_t

                                @blk.scalar
                                def _(s):
                                    s.wait_ge(s_pe.h, last_avs[st])
                                    if norm_sc[1 - stp]:
                                        s.wait_ge(s_v.h, norm_sc[1 - stp])
                                    if kind == "a":
                                        s.activation(ctxf[:, 0:256],
                                                     psC[stp][:, 0:256],
                                                     AF.Copy).then_inc(s_sc.h, 1)
                                    else:
                                        s.activation(ctxf[0:64, 0:256],
                                                     psC[stp][0:64, 0:256],
                                                     AF.Copy)
                                        s.activation(ctxf[64:128, 0:256],
                                                     psC[stp][0:64, 256:512],
                                                     AF.Copy).then_inc(s_sc.h, 1)
                                cf_t = s_sc.inc()

                                @blk.tensor
                                def _(t):
                                    t.wait_ge(s_v.h, rr_t)
                                    t.wait_ge(s_sc.h, cf_t)
                                    if kind == "a":
                                        t.matmul(psC[stp][:, 0:256], onesr_bf[:],
                                                 rr[0:1, 0:256], start=True,
                                                 stop=True).then_inc(s_pe.h, 1)
                                    else:
                                        t.matmul(psC[stp][:, 0:512], onesr_bf[:],
                                                 rr[0:1, 0:512], start=True,
                                                 stop=True).then_inc(s_pe.h, 1)
                                bc_t = s_pe.inc()
                                bc_prev[0] = bc_t

                                @blk.vector
                                def _(v):
                                    v.wait_ge(s_pe.h, bc_t)
                                    if st == 0 and ctx_dma:
                                        v.wait_ge(s_gp.h, ctx_dma)
                                    if kind == "a":
                                        v.tensor_tensor(
                                            ctxs[:, st * 256:st * 256 + 256],
                                            ctxf[:, 0:256], psC[stp][:, 0:256],
                                            AL.mult).then_inc(s_v.h, 1)
                                    else:
                                        v.tensor_tensor(
                                            ctxs[0:64, st * 256:st * 256 + 256],
                                            ctxf[0:64, 0:256],
                                            psC[stp][0:64, 0:256], AL.mult)
                                        v.tensor_tensor(
                                            ctxs[64:128, st * 256:st * 256 + 256],
                                            ctxf[64:128, 0:256],
                                            psC[stp][64:128, 256:512],
                                            AL.mult).then_inc(s_v.h, 1)
                                n_t = s_v.inc()
                                cb_free[stp] = n_t
                                norm_sc[stp] = n_t

                            norm_all = norm_sc[(NST - 1) % 2]
                            row0 = u * 128 if kind == "a" else (u * 4 + jp * 2) * 64

                            @blk.gpsimd
                            def _(g):
                                g.wait_ge(s_v.h, norm_all)
                                g.dma_start(out=c_agi.ap()[row0:row0 + 128, :],
                                            in_=ctxs[:]).then_inc(s_gp.h, 16)
                            ctx_dma = s_gp.inc(16)
                            cago_dmas.append(ctx_dma)
                            if dbg and sub_idx == 0 and u == 0 and jp == 0:
                                @blk.gpsimd
                                def _(g):
                                    g.dma_start(out=t_qh_d, in_=qh[:]).then_inc(
                                        s_gp.h, 16)
                                    g.dma_start(out=t_kh_d, in_=kh[:]).then_inc(
                                        s_gp.h, 16)
                                    g.dma_start(out=t_vh_d, in_=vh[:]).then_inc(
                                        s_gp.h, 16)
                                    g.dma_start(out=t_cx_d, in_=ctxs[:]).then_inc(
                                        s_gp.h, 16)
                                    g.dma_start(out=t_p_d, in_=p_sb[:]).then_inc(
                                        s_gp.h, 16)
                                s_gp.inc(16 * 5)

                        slab_free[par] = q_pe_last

                    last_ctx_dma = cago_dmas[-1]

                barrier()
                with ExitStack() as octx:
                    ops = lambda n, s: octx.enter_context(
                        nc.psum_tensor(n + sfx, s, F32))
                    ctxrhs = octx.enter_context(
                        nc.sbuf_tensor("ctxrhs" + sfx, [128, DCH * 1024], BF16))
                    psO = [ops("psO0", [128, 512]), ops("psO1", [128, 512])]

                    @blk.gpsimd
                    def _(g):
                        g.wait_ge(s_gp.h, last_ctx_dma)
                        if sub_idx == 2:
                            g.wait_ge(s_sd.h, cago_read[0])
                        if dbg and sub_idx == 0:
                            g.dma_start(out=t_ci_d, in_=c_agi.ap()).then_inc(
                                s_gp.h, 16)
                        g.collective_compute(
                            "AllGather", AL.bypass, replica_groups=RG_PAIR,
                            ins=[c_agi.ap().opt()],
                            outs=[c_ago.ap().opt()]).then_inc(s_cc.h, 1)
                    cag = s_cc.inc()
                    if dbg and sub_idx == 0:
                        s_gp.inc(16)

                    if sub_idx == 0 and len(wag_tgts) == 1 and not no_quant:
                        emit_wag(1)
                        emit_wag(2)

                    @blk.scalar
                    def _(s):
                        s.wait_ge(s_cc.h, cag)
                        s.wait_ge(s_in.h, tok_tgt)
                        reg = s.alloc_register()
                        s.reg_load(reg, tok0_sb[0:1, 0:1])
                        rvs = [RuntimeValue(reg, min_val=0, max_val=TLOC)]
                        cgr = c_ago.ap().rearrange("(cc p) t -> p cc t", p=128)
                        orr = ctxrhs[:].rearrange("p (cc t) -> p cc t", t=1024)
                        for q4 in range(4):
                            s.dma_start(
                                out=orr[:, q4 * 4:(q4 + 1) * 4, :],
                                in_=cgr[:, q4 * 4:(q4 + 1) * 4,
                                        bass.ds(rvs[0], 1024)]
                                ).then_inc(s_sd.h, 16)
                    rhs_tgt = s_sd.inc(16 * 4)
                    cago_read[0] = rhs_tgt
                    if dbg and sub_idx == 0:
                        @blk.gpsimd
                        def _(g):
                            g.wait_ge(s_sd.h, rhs_tgt)
                            g.dma_start(out=t_cr_d, in_=ctxrhs[:]).then_inc(
                                s_gp.h, 16)
                        s_gp.inc(16)

                    wo_d = woa_d if kind == "a" else wob_d
                    oslab_free = list(slab_free)
                    opp = [0, 0]
                    for oc in range(DCH):
                        par = oc % 2

                        @blk.gpsimd
                        def _(g):
                            if oslab_free[par]:
                                g.wait_ge(s_pe.h, oslab_free[par])
                            wv = wslb[par][:].rearrange("p (c q) -> p c q", q=448)
                            g.dma_start(
                                out=wv[:, :, 0:128],
                                in_=wo_d.rearrange("(c p) o -> p c o", p=128)
                                [:, :, oc * 128:(oc + 1) * 128]).then_inc(
                                    s_gp.h, 16)
                        cast_tgt = s_gp.inc(16)

                        @blk.gpsimd
                        def _(g):
                            g.wait_ge(s_gp.h, cast_tgt)
                            g.tensor_copy(echo[:, 0:16],
                                          wslb[par][:, (DCH - 1) * 448 + 112:
                                                    (DCH - 1) * 448 + 128]
                                          ).then_inc(s_qc.h, 1)
                        echo_t = s_qc.inc()

                        pe_ts = []
                        for tt in range(2):
                            bp = (2 * oc + tt) % 2

                            @blk.tensor
                            def _(t):
                                if tt == 0:
                                    t.wait_ge(s_qc.h, echo_t)
                                    if oc == 0:
                                        t.wait_ge(s_sd.h, rhs_tgt)
                                if opp[bp]:
                                    t.wait_ge(s_v.h, opp[bp])
                                last = None
                                for cc in range(DCH):
                                    last = t.matmul(
                                        psO[bp][:, 0:512],
                                        wslb[par][:, cc * 448:cc * 448 + 128],
                                        ctxrhs[:, cc * 1024 + tt * 512:
                                               cc * 1024 + tt * 512 + 512],
                                        start=(cc == 0), stop=(cc == DCH - 1))
                                last.then_inc(s_pe.h, 1)
                            pe_t = s_pe.inc()
                            pe_ts.append(pe_t)

                            @blk.vector
                            def _(v):
                                v.wait_ge(s_pe.h, pe_t)
                                xsl = xt[:, oc * TLOC + tt * 512:
                                         oc * TLOC + tt * 512 + 512]
                                if dbg and sub_idx == 0 and oc == 0 and tt == 0:
                                    v.tensor_copy(tmpr[:, 0:512],
                                                  psO[bp][:, 0:512])
                                if dbg and sub_idx == 0 and oc == 15 and tt == 1:
                                    v.tensor_copy(tmpr[:, 0:512],
                                                  psO[bp][:, 0:512])
                                v.scalar_tensor_tensor(
                                    xsl, psO[bp][:, 0:512], 1.0, xsl,
                                    AL.mult, AL.add).then_inc(s_v.h, 1)
                            res_last = s_v.inc()
                            opp[bp] = res_last
                            if dbg and sub_idx == 0 and oc == 0 and tt == 0:
                                @blk.gpsimd
                                def _(g):
                                    g.wait_ge(s_v.h, res_last)
                                    g.dma_start(out=t_o_d,
                                                in_=tmpr[:, 0:512]).then_inc(
                                                    s_gp.h, 16)
                                    g.dma_start(out=t_wo_d,
                                                in_=wslb[0][:]).then_inc(
                                                    s_gp.h, 16)
                                    g.dma_start(out=t_xm_d,
                                                in_=xt[:, 0:512]).then_inc(
                                                    s_gp.h, 16)
                                s_gp.inc(48)
                            if dbg and sub_idx == 0 and oc == 15 and tt == 1:
                                @blk.gpsimd
                                def _(g):
                                    g.wait_ge(s_v.h, res_last)
                                    g.dma_start(out=t_o2_d,
                                                in_=tmpr[:, 0:512]).then_inc(
                                                    s_gp.h, 16)
                                s_gp.inc(16)
                        oslab_free[par] = pe_ts[-1]
                return res_last

        def emit_ffn(sub_idx, pre_v):
            sfx = f"_s{sub_idx}"
            with ExitStack() as fctx:
                fsb = lambda n, s, d: fctx.enter_context(
                    nc.sbuf_tensor(n + sfx, s, d))
                fps = lambda n, s: fctx.enter_context(
                    nc.psum_tensor(n + sfx, s, F32))
                y_own = fsb("y_own", [128, DCH * TLOC], BF16)
                hT = fsb("hT", [128, HCH * 512], BF16)
                w1sl = [fsb("w1sl0", [128, DCH * 128], BF16),
                        fsb("w1sl1", [128, DCH * 128], BF16)]
                w2sl = [fsb("w2sl0", [128, DCH * 128], BF16),
                        fsb("w2sl1", [128, DCH * 128], BF16)]
                w3sl = [fsb("w3sl0", [128, HCH * 128], BF16),
                        fsb("w3sl1", [128, HCH * 128], BF16)]
                sg = fsb("sg", [128, 512], F32)

                with ExitStack() as lctx:
                    lps = lambda n, s: lctx.enter_context(
                        nc.psum_tensor(n + sfx, s, F32))
                    psS0 = lps("psS0", [1, 512])
                    psS1 = lps("psS1", [1, 512])
                    psB0 = lps("psB0", [128, 512])
                    psB1 = lps("psB1", [128, 512])
                    apply_last, _, _ = emit_ln(hT, psS0, psS1, psB0, psB1, pre_v,
                                               to_agi=False, own_dst=y_own)

                psX1 = [fps("psX10", [128, 512]), fps("psX11", [128, 512])]
                psX2 = [fps("psX20", [128, 512]), fps("psX21", [128, 512])]
                psF = [fps("psF0", [128, 512]), fps("psF1", [128, 512])]

                if dbg and sub_idx == 1:
                    @blk.gpsimd
                    def _(g):
                        if wag_tgts:
                            g.wait_ge(s_cc.h, wag_tgts[-1])
                        g.dma_start(out=t_thr_d, in_=thrv[:, 0:8]).then_inc(
                            s_gp.h, 16)
                        g.dma_start(out=t_wq_d,
                                    in_=wqo[0].ap()[0:128, :]).then_inc(
                                        s_gp.h, 16)
                    s_gp.inc(32)

                w1r = wqo[0].ap().rearrange("(c p) f -> p c f", p=128)
                w2r = wqo[1].ap().rearrange("(c p) f -> p c f", p=128)
                w3r = wqo[2].ap().rearrange("(c p) f -> p c f", p=128)

                res_last = 0
                x12_free = [0, 0]
                sg_free = [0, 0]
                h_free = [0, 0]
                f_free = [0, 0]
                frv_free = [0, 0]
                f_last_prev_tt = 0
                for tt in range(2):
                    for hc in range(HCH):
                        par = hc % 2

                        @blk.sync
                        def _(s):
                            if tt == 0 and hc == 0 and len(wag_tgts) == 3:
                                s.wait_ge(s_cc.h, wag_tgts[1])
                            if x12_free[par]:
                                s.wait_ge(s_pe.h, x12_free[par])
                            s.dma_start(out=w1sl[par][:].rearrange(
                                            "p (c f) -> p c f", f=128),
                                        in_=w1r[:, :, hc * 128:(hc + 1) * 128]
                                        ).then_inc(s_fw[par].h, 16)
                            s.dma_start(out=w2sl[par][:].rearrange(
                                            "p (c f) -> p c f", f=128),
                                        in_=w2r[:, :, hc * 128:(hc + 1) * 128]
                                        ).then_inc(s_fw[par].h, 16)
                        wd = s_fw[par].inc(32)

                        @blk.gpsimd
                        def _(g):
                            g.wait_ge(s_fw[par].h, wd)
                            g.tensor_copy(echo[:, 0:16],
                                          w1sl[par][:, DCH * 128 - 16:DCH * 128])
                            g.tensor_copy(echo[:, 0:16],
                                          w2sl[par][:, DCH * 128 - 16:DCH * 128]
                                          ).then_inc(s_qc.h, 1)
                        wde = s_qc.inc()

                        @blk.tensor
                        def _(t):
                            t.wait_ge(s_qc.h, wde)
                            if tt == 0 and hc == 0:
                                t.wait_ge(s_v.h, apply_last)
                            if sg_free[par]:
                                t.wait_ge(s_sc.h, sg_free[par])
                            last = None
                            for c in range(DCH):
                                last = t.matmul(
                                    psX1[par][:, 0:512],
                                    w1sl[par][:, c * 128:(c + 1) * 128],
                                    y_own[:, c * TLOC + tt * 512:
                                          c * TLOC + tt * 512 + 512],
                                    start=(c == 0), stop=(c == DCH - 1))
                            last.then_inc(s_pe.h, 1)
                        x1_t = s_pe.inc()

                        @blk.scalar
                        def _(s):
                            s.wait_ge(s_pe.h, x1_t)
                            for hf in h_free:
                                if hf:
                                    s.wait_ge(s_v.h, hf)
                            s.activation(sg[:, 0:512],
                                         psX1[par][:, 0:512],
                                         AF.Sigmoid).then_inc(s_sc.h, 1)
                        sg_t = s_sc.inc()
                        sg_free[par] = sg_t

                        @blk.tensor
                        def _(t):
                            if h_free[par]:
                                t.wait_ge(s_v.h, h_free[par])
                            last = None
                            for c in range(DCH):
                                last = t.matmul(
                                    psX2[par][:, 0:512],
                                    w2sl[par][:, c * 128:(c + 1) * 128],
                                    y_own[:, c * TLOC + tt * 512:
                                          c * TLOC + tt * 512 + 512],
                                    start=(c == 0), stop=(c == DCH - 1))
                            last.then_inc(s_pe.h, 1)
                        x2_t = s_pe.inc()
                        x12_free[par] = x2_t

                        @blk.vector
                        def _(v):
                            v.wait_ge(s_pe.h, x2_t)
                            v.wait_ge(s_sc.h, sg_t)
                            if tt == 1 and hc == 0:
                                v.wait_ge(s_pe.h, f_last_prev_tt)
                            if dbg and sub_idx == 1 and tt == 0 and hc == 0:
                                v.tensor_copy(tmpr[:, 0:512], psX1[par][:, 0:512])
                            v.tensor_tensor(hT[:, hc * 512:(hc + 1) * 512],
                                            sg[:, 0:512],
                                            psX2[par][:, 0:512],
                                            AL.mult).then_inc(s_v.h, 1)
                        h_free[par] = s_v.inc()
                        if dbg and sub_idx == 1 and tt == 0 and hc == 0:
                            @blk.gpsimd
                            def _(g):
                                g.wait_ge(s_v.h, h_free[par])
                                g.dma_start(out=t_x1_d,
                                            in_=tmpr[:, 0:512]).then_inc(
                                                s_gp.h, 16)
                                g.dma_start(out=t_h_d,
                                            in_=hT[:, 0:512]).then_inc(
                                                s_gp.h, 16)
                            s_gp.inc(32)

                    h_all = h_free[(HCH - 1) % 2]
                    for oc in range(DCH):
                        par = oc % 2

                        @blk.sync
                        def _(s):
                            if tt == 0 and oc == 0 and len(wag_tgts) == 3:
                                s.wait_ge(s_cc.h, wag_tgts[2])
                            if f_free[par]:
                                s.wait_ge(s_pe.h, f_free[par])
                            for q in range(4):
                                s.dma_start(
                                    out=w3sl[par][:].rearrange(
                                        "p (c f) -> p c f",
                                        f=128)[:, q * 11:(q + 1) * 11, :],
                                    in_=w3r[:, q * 11:(q + 1) * 11,
                                            oc * 128:(oc + 1) * 128]
                                    ).then_inc(s_f3[par].h, 16)
                        w3d = s_f3[par].inc(64)

                        @blk.gpsimd
                        def _(g):
                            g.wait_ge(s_f3[par].h, w3d)
                            g.tensor_copy(echo[:, 0:16],
                                          w3sl[par][:, HCH * 128 - 16:HCH * 128]
                                          ).then_inc(s_qc.h, 1)
                        w3e = s_qc.inc()

                        @blk.tensor
                        def _(t):
                            t.wait_ge(s_qc.h, w3e)
                            if oc == 0:
                                t.wait_ge(s_v.h, h_all)
                            if frv_free[par]:
                                t.wait_ge(s_v.h, frv_free[par])
                            last = None
                            for hc in range(HCH):
                                last = t.matmul(
                                    psF[par][:, 0:512],
                                    w3sl[par][:, hc * 128:(hc + 1) * 128],
                                    hT[:, hc * 512:(hc + 1) * 512],
                                    start=(hc == 0), stop=(hc == HCH - 1))
                            last.then_inc(s_pe.h, 1)
                        f_t = s_pe.inc()
                        f_free[par] = f_t

                        @blk.vector
                        def _(v):
                            v.wait_ge(s_pe.h, f_t)
                            xsl = xt[:, oc * TLOC + tt * 512:
                                     oc * TLOC + tt * 512 + 512]
                            v.scalar_tensor_tensor(
                                xsl, psF[par][:, 0:512], 1.0, xsl,
                                AL.mult, AL.add).then_inc(s_v.h, 1)
                        frv_free[par] = s_v.inc()
                        res_last = frv_free[par]
                    f_last_prev_tt = f_free[(DCH - 1) % 2]
                return res_last

        yag1_tgt = [0]
        gout_tgt = [0]
        pre_v = [(s_in.h, xt_tgt), (s_v.h, consts_tgt), (s_in.h, mask_tgt)]
        res = 0
        for sub in range(n_sub):
            if sub == 0:
                res = emit_attn("a", 0, pre_v)
            elif sub == 1:
                res = emit_ffn(1, [])
            elif sub == 2:
                res = emit_attn("b", 2, [])
            else:
                res = emit_ffn(3, [])
            if dbg:
                @blk.gpsimd
                def _(g):
                    g.wait_ge(s_v.h, res)
                    for c in range(DCH):
                        g.dma_start(out=dbg_d[sub][c * 128:(c + 1) * 128, :],
                                    in_=xt[:, c * TLOC:(c + 1) * TLOC]
                                    ).then_inc(s_gp.h, 16)
                s_gp.inc(16 * DCH)

                @blk.gpsimd
                def _(g):
                    g.wait_ge(s_gp.h, s_gp.v)
            barrier()
            pre_v = []

        @blk.gpsimd
        def _(g):
            g.wait_ge(s_v.h, res)
            for c in range(DCH):
                g.dma_start(out=out_d[c * 128:(c + 1) * 128, :],
                            in_=xt[:, c * TLOC:(c + 1) * TLOC]).then_inc(s_gp.h, 16)
        final_tgt = s_gp.inc(16 * DCH)

        @blk.gpsimd
        def _(g):
            g.wait_ge(s_gp.h, final_tgt)
    return nc


def make_inputs(L0_attn_a_x, L0_attn_a_w_project, L0_attn_a_w_out, L0_attn_b_Wq,
                L0_attn_b_Wk, L0_attn_b_Wv, L0_attn_b_Wo, L0_ffn_w1, L0_ffn_w2,
                L0_ffn_w3):
    x = np.asarray(L0_attn_a_x, dtype=np.float32)
    wpT = np.ascontiguousarray(np.asarray(L0_attn_a_w_project, np.float32).T)
    woaT = np.ascontiguousarray(np.asarray(L0_attn_a_w_out, np.float32).T)
    wqT = np.ascontiguousarray(np.asarray(L0_attn_b_Wq, np.float32).T)
    wkT = np.ascontiguousarray(np.asarray(L0_attn_b_Wk, np.float32).T)
    wvT = np.ascontiguousarray(np.asarray(L0_attn_b_Wv, np.float32).T)
    wobT = np.ascontiguousarray(np.asarray(L0_attn_b_Wo, np.float32).T)
    w1T = np.ascontiguousarray(np.asarray(L0_ffn_w1, np.float32).T)
    w2T = np.ascontiguousarray(np.asarray(L0_ffn_w2, np.float32).T)
    w3T = np.ascontiguousarray(np.asarray(L0_ffn_w3, np.float32).T)

    tri = (np.arange(128)[:, None] <= np.arange(128)[None, :])
    maskA = np.concatenate([tri, np.ones((128, 128), bool)], axis=1)
    maskB = np.concatenate([np.zeros((128, 128), bool), tri], axis=1)
    maskab = np.concatenate([maskA, maskB], axis=1).astype(ml_dtypes.bfloat16)

    in_maps = []
    for c in range(8):
        s, h = c // 2, c % 2
        t0, t1 = h * TLOC, (h + 1) * TLOC
        kcols = []
        for g in range(4 * h, 4 * h + 4):
            kg = wkT[:, g * 64:(g + 1) * 64]
            kcols += [kg, kg]
        wkb = np.ascontiguousarray(np.concatenate(kcols, axis=1))
        wvb = np.ascontiguousarray(wvT[:, h * 256:(h + 1) * 256])
        in_maps.append({
            "xt": np.ascontiguousarray(x[s].T[:, t0:t1]),
            "tok0": np.array([[t0]], dtype=np.uint32),
            "wpq": np.ascontiguousarray(wpT[:, t0:t1]),
            "wpk": np.ascontiguousarray(wpT[:, 2048 + t0:2048 + t1]),
            "wpv": np.ascontiguousarray(wpT[:, 4096 + t0:4096 + t1]),
            "woa": woaT,
            "wqb": np.ascontiguousarray(wqT[:, t0:t1]),
            "wkb": wkb,
            "wvb": wvb,
            "wob": wobT,
            "w1s": np.ascontiguousarray(w1T[256 * c:256 * (c + 1), :]),
            "w2s": np.ascontiguousarray(w2T[256 * c:256 * (c + 1), :]),
            "w3s": np.ascontiguousarray(w3T[704 * c:704 * (c + 1), :]),
            "maskab": maskab,
        })
    return in_maps


_NC_CACHE = {}


def kernel(**inputs):
    key = (4, False)
    if key not in _NC_CACHE:
        _NC_CACHE[key] = build(n_sub=4, dbg=False)
    nc = _NC_CACHE[key]
    in_maps = make_inputs(**inputs)
    res = run_bass_kernel_spmd(nc, in_maps, core_ids=list(range(8)))
    out = np.empty((4, 2048, 2048), dtype=np.float32)
    for c in range(8):
        s, h = c // 2, c % 2
        out[s, h * TLOC:(h + 1) * TLOC, :] = res.results[c]["out_xt"].T
    return out
